# version 34
# baseline (speedup 1.0000x reference)
import contextlib
import math
import numpy as np

import concourse.bass as bass
import concourse.mybir as mybir
from concourse.ap import AP
from concourse.bass_utils import run_bass_kernel_spmd

F32 = mybir.dt.float32
BF16 = mybir.dt.bfloat16
I32 = mybir.dt.int32
AF = mybir.ActivationFunctionType
ALU = mybir.AluOpType
AX = mybir.AxisListType

D = 1024
KC = 8
H = 16
DH = 64
CHL = 64
TILE = 256
CPT = TILE // CHL
FFN = 4096
HID3 = 3072
QKD = 128
HID2 = 2048
LN_EPS = 1e-5
ZCLAMP = 44.0
NROW = 21
A_STAGE = 9
A_SUB = 9
A_SUB4 = 9

ENGINES = ("pe", "dve", "act", "pool", "sync")



def _t5_bucket(dd):
    NUM_BUCKETS, MAX_DIST = 32, 128
    max_exact = NUM_BUCKETS // 2
    if dd < max_exact:
        return dd
    ln_ratio = math.log(MAX_DIST / max_exact)
    vv = max_exact + int(math.log(max(dd, 1) / max_exact) / ln_ratio
                         * (NUM_BUCKETS - max_exact))
    return min(vv, NUM_BUCKETS - 1)


def host_statics():
    st = {}
    st["ident"] = np.eye(128, dtype=np.float32)
    m = np.zeros((64, 64), np.float32)
    for tp in range(64):
        m[tp, tp:] = 1.0
    st["triu64"] = m
    hm = np.zeros((16, KC, 128), np.float32)
    for kc in range(KC):
        hm[2 * kc, kc, 0:64] = 1.0
        hm[2 * kc + 1, kc, 64:128] = 1.0
    st["headmap"] = hm
    sq = np.zeros((128, KC, 16), np.float32)
    for kc in range(KC):
        sq[0:64, kc, 2 * kc] = 1.0
        sq[64:128, kc, 2 * kc + 1] = 1.0
    st["sqmap"] = sq
    oh = np.zeros((32, 512), np.float32)
    s = math.sqrt(128.0)
    for k in range(511):
        oh[_t5_bucket(max(k - 255, 0)), k] = s
    st["t5oh"] = oh
    rv = np.zeros((128, 128), np.float32)
    for p in range(128):
        rv[p, 127 - p] = 1.0
    st["rev"] = rv
    mk = np.zeros((128, 2, 256), np.float32)
    for jt in range(2):
        for p in range(128):
            mk[p, jt, :jt * 128 + p] = -1e9
    st["gmask"] = mk
    return st


def host_prep(inputs):
    sh = dict(host_statics())
    w = inputs
    cp = np.ascontiguousarray
    for k in ("L0_attn_Wq", "L0_attn_Wk", "L0_attn_Wv", "L0_attn_Wg", "L0_attn_Wo",
              "L0_attn_W1", "L0_attn_W2", "L1_attn_Wh", "L1_attn_Wqk", "L1_attn_Wout"):
        sh[k] = cp(w[k])
    for L in ("L0", "L1"):
        sh[f"{L}_wgT"] = cp(w[f"{L}_ffn_w_gate"].T)
        sh[f"{L}_wuT"] = cp(w[f"{L}_ffn_w_up"].T)
        sh[f"{L}_wdT"] = cp(w[f"{L}_ffn_w_down"].T)
    sh["L1_attn_rel_table"] = cp(w["L1_attn_rel_table"])
    sh["qkwT"] = cp(w["L1_attn_qk_w"].T)
    sh["qkbT"] = cp(w["L1_attn_qk_b"].T)
    for k in ("L0_attn_bg", "L0_attn_ln1_w", "L0_attn_ln1_b", "L0_attn_ln2_w",
              "L0_attn_ln2_b", "L0_attn_b1", "L0_attn_b2", "L1_attn_ln_w",
              "L1_attn_ln_b", "L1_attn_bh", "L1_attn_bqk", "L1_attn_bout"):
        a = w[k]
        sh[k + "_pc"] = cp(a.reshape(-1, 128).T)

    x = w["L0_attn_x"]
    seq = x.shape[1]
    nt = seq // TILE
    per_core = []
    for b in range(x.shape[0]):
        xt = cp(x[b].T).reshape(KC, 128, nt, TILE).transpose(2, 1, 0, 3)
        per_core.append({"x_in": cp(xt)})
    return sh, per_core



class MK:
    def __init__(self, seq, phases, scratch_out, stack):
        self.nc = bass.Bass("TRN2", target_bir_lowering=False)
        self.seq = seq
        self.nt = seq // TILE
        self.phases = phases
        self.scratch_out = set(scratch_out)
        self.stack = stack
        self.entries = {e: [] for e in ENGINES}
        self.cnt = {}
        self.semh = {}
        self.last_wait = {e: {} for e in ENGINES}
        self._declare_dram()
        self._alloc_global()

    def _sem(self, name):
        self.semh[name] = self.stack.enter_context(self.nc.semaphore(name))
        self.cnt[name] = 0

    AUTO = ("dve", "act", "pool")

    def emit(self, eng, make, inc=None, n=1):
        if eng in self.AUTO:
            self.wait(eng, eng, self.cnt[eng])
            inc = eng
            n = 1
        if inc is not None:
            self.cnt[inc] += n
            self.entries[eng].append(("op", make, self.semh[inc], n))
        else:
            self.entries[eng].append(("op", make, None, 0))
        return self.cnt[inc] if inc else None

    def dma(self, eng, make, q):
        if eng in self.AUTO:
            self.wait(eng, eng, self.cnt[eng])
        self.cnt[q] += 16
        self.entries[eng].append(("op", make, self.semh[q], 16))
        return self.cnt[q]

    def wait(self, eng, sem, val=None):
        if val is None:
            val = self.cnt[sem]
        if val <= 0:
            return
        if val <= self.last_wait[eng].get(sem, 0):
            return
        self.last_wait[eng][sem] = val
        self.entries[eng].append(("wait", self.semh[sem], val, 0))

    def barrier(self):
        snap = dict(self.cnt)
        for e in ENGINES:
            for s, v in snap.items():
                if v > 0:
                    self.wait(e, s, v)

    def sb(self, name, shape, dtype, stk=None):
        return (stk or self.stack).enter_context(self.nc.sbuf_tensor(name, shape, dtype))

    def ps(self, name, shape, dtype=F32, stk=None):
        return (stk or self.stack).enter_context(self.nc.psum_tensor(name, shape, dtype))

    def row1(self, slot):
        return self.rowpool[0:1, slot * TILE:(slot + 1) * TILE]

    def row16(self, slot):
        return self.rowpool[0:16, slot * TILE:(slot + 1) * TILE]

    def _declare_dram(self):
        nc = self.nc
        nt = self.nt
        self.d = {}

        def din(name, shape, dtype=F32):
            self.d[name] = nc.dram_tensor(name, shape, dtype, kind="ExternalInput").ap()

        din("x_in", [nt, 128, KC, TILE])
        din("ident", [128, 128])
        din("triu64", [64, 64])
        din("headmap", [16, KC, 128])
        din("sqmap", [128, KC, 16])
        din("t5oh", [32, 512])
        din("rev", [128, 128])
        din("gmask", [128, 2, 256])
        for k in ("L0_attn_Wq", "L0_attn_Wk", "L0_attn_Wv", "L0_attn_Wg", "L0_attn_Wo"):
            din(k, [1024, 1024])
        din("L0_attn_W1", [1024, 4096])
        din("L0_attn_W2", [4096, 1024])
        for L in ("L0", "L1"):
            din(f"{L}_wgT", [1024, HID3], I32)
            din(f"{L}_wuT", [1024, HID3], I32)
            din(f"{L}_wdT", [HID3, 1024], I32)
        din("L1_attn_Wh", [1024, 4096])
        din("L1_attn_Wqk", [1024, QKD])
        din("L1_attn_Wout", [HID2, 1024])
        for k in ("L0_attn_bg", "L0_attn_ln1_w", "L0_attn_ln1_b", "L0_attn_ln2_w",
                  "L0_attn_ln2_b", "L0_attn_b2", "L1_attn_ln_w", "L1_attn_ln_b",
                  "L1_attn_bout"):
            din(k + "_pc", [128, KC])
        din("L0_attn_b1_pc", [128, 32])
        din("L1_attn_bh_pc", [128, 32])
        din("L1_attn_bqk_pc", [128, 1])
        din("L1_attn_rel_table", [32])
        din("qkwT", [QKD, 4])
        din("qkbT", [QKD, 4])

        def scratch(name, shape):
            kind = "ExternalOutput" if name in self.scratch_out else "Internal"
            self.d[name] = nc.dram_tensor(name, shape, F32, kind=kind).ap()

        self.d["biasT_d"] = nc.dram_tensor(
            "biasT_d", [128, 2, 256], BF16, kind="Internal").ap()
        for name in ("xn_s", "xa_s", "s_s", "x1_s", "xn1_s", "s1_s"):
            scratch(name, [nt, 128, KC, TILE])
        self.d["mid_s"] = nc.dram_tensor(
            "mid_s", [nt, 128, HID3 // 128, TILE], BF16, kind="Internal").ap()
        self.d["u_s"] = nc.dram_tensor(
            "u_s", [nt // 2, 128, 32, 2 * TILE], BF16, kind="Internal").ap()
        self.d["out"] = nc.dram_tensor("out", [nt, 128, KC, TILE], F32,
                                       kind="ExternalOutput").ap()
        self.d["fpad_d"] = nc.dram_tensor("fpad_d", [1, 640], F32, kind="Internal").ap()

    def _alloc_global(self):
        for s in ("pe", "dve", "act", "pool", "qs", "qp",
                  "ql0", "ql1", "qh0", "qh1", "qw0", "qw1", "qx0", "qx1",
                  "qp0", "qp1", "qt0", "qt1"):
            self._sem(s)
        sb = self.sb
        self.ident_b = sb("ident_b", [128, 128], BF16)
        self.triu_b = sb("triu_b", [128, 64], BF16)
        self.hm_b = sb("hm_b", [16, KC, 128], BF16)
        self.sq_b = sb("sq_b", [128, KC, 16], BF16)
        self.ones1_b = sb("ones1_b", [128, 1], BF16)
        self.ones1r_b = sb("ones1r_b", [1, 128], BF16)
        self.rbf = sb("rbf_hilo", [16, 2, TILE], BF16)
        self.zeros64 = sb("zeros64", [128, 64], F32)
        self.epsr = sb("epsr", [1, 1], F32)
        self.rel_f = sb("rel_f", [32, 1], F32)
        self.vec = {}
        for name, ncol in (("bg", KC), ("ln1w", KC), ("ln1b", KC), ("ln2w", KC),
                           ("ln2b", KC), ("b2", KC), ("b1", 32), ("lnw", KC),
                           ("lnb", KC), ("bh", 32), ("bqk", 1), ("bout", KC),
                           ("qkw", 4), ("qkb", 4)):
            self.vec[name] = sb("vec_" + name, [128, ncol], F32)
        self.S_f = sb("S_f", [128, 512], F32)
        self.S_b = sb("S_b", [128, 512], BF16)
        self.xbf_g = sb("xbf_g", [128, KC, TILE], BF16)
        self.sqb_g = sb("sqb_g", [128, KC, TILE], BF16)
        self.tmpf_g = sb("tmpf_g", [128, KC, TILE], F32)
        self.rowpool = sb("rowpool", [16, NROW * TILE], F32)
        self.c44 = sb("c_44", [128, 1], F32)
        self.cm44 = sb("c_m44", [128, 1], F32)
        self.c1e6 = sb("c_1e6", [128, 1], F32)
        self.c1e24 = sb("c_1e24", [128, 1], F32)
        self.chalf = sb("c_half", [128, 1], F32)

    def preamble(self):
        d = self.d
        v = self.vec

        def ld(eng, dst, src, q):
            self.dma(eng, lambda E, dst=dst, src=src: E.dma_start(out=dst, in_=src), q)

        for t, val in ((self.zeros64, 0.0), (self.epsr, LN_EPS), (self.S_f, 0.0),
                       (self.S_b, 0.0),
                       (self.ones1_b, 1.0), (self.ones1r_b, 1.0),
                       (self.c44, ZCLAMP), (self.cm44, -ZCLAMP), (self.c1e6, 1e-6),
                       (self.c1e24, 1e-24), (self.chalf, 0.5)):
            self.emit("pool", lambda E, t=t, val=val: E.memset(t[:], val), inc="pool")
        ld("sync", self.rel_f[:], d["L1_attn_rel_table"].rearrange("(r o) -> r o", o=1), "qs")
        ld("pool", self.ident_b[:], d["ident"], "qp")
        ld("pool", self.triu_b[0:64, :], d["triu64"], "qp")
        ld("pool", self.triu_b[64:128, :], d["triu64"], "qp")
        ld("pool", self.hm_b[:], d["headmap"], "qp")
        ld("pool", self.sq_b[:], d["sqmap"], "qp")
        vecsrc = {"bg": "L0_attn_bg", "ln1w": "L0_attn_ln1_w", "ln1b": "L0_attn_ln1_b",
                  "ln2w": "L0_attn_ln2_w", "ln2b": "L0_attn_ln2_b", "b2": "L0_attn_b2",
                  "b1": "L0_attn_b1", "lnw": "L1_attn_ln_w", "lnb": "L1_attn_ln_b",
                  "bh": "L1_attn_bh", "bout": "L1_attn_bout"}
        for name, src in vecsrc.items():
            ld("sync", v[name][:], d[src + "_pc"], "qs")
        ld("sync", v["bqk"][:], d["L1_attn_bqk_pc"], "qs")
        ld("sync", v["qkw"][:], d["qkwT"], "qs")
        ld("sync", v["qkb"][:], d["qkbT"], "qs")
        self.barrier()

        with contextlib.ExitStack() as st:
            t5oh_f = self.sb("t5oh_f", [32, 512], F32, st)
            rev_b = self.sb("rev_b", [128, 128], BF16, st)
            gmask_f = self.sb("gmask_f", [128, 2, 256], F32, st)
            frow = self.sb("fpad_row", [1, 640], F32, st)
            toep = self.sb("toep", [128, 2, 256], BF16, st)
            pf = self.ps("p_fpad", [1, 512], F32, st)
            pb = self.ps("p_bias", [128, 512], F32, st)
            ld("sync", t5oh_f[:], d["t5oh"], "qs")
            ld("sync", gmask_f[:], d["gmask"], "qs")
            ld("pool", rev_b[:], d["rev"], "qp")
            self.emit("pool", lambda E: E.memset(frow[:], 0.0), inc="pool")
            self.barrier()
            self.emit("pe", lambda E: E.matmul(pf[:], self.rel_f[:], t5oh_f[:],
                                               start=True, stop=True), inc="pe")
            self.wait("dve", "pe")
            self.emit("dve", lambda E: E.tensor_copy(frow[:, 0:512], pf[:]), inc="dve")
            self.wait("sync", "dve")
            self.dma("sync", lambda E: E.dma_start(out=d["fpad_d"], in_=frow[:]), "qs")
            for J in range(2):
                sap = AP(d["fpad_d"].tensor, 128 - J * 128, [[1, 128], [1, 256]])
                self.wait("pool", "qs")
                self.dma("pool", lambda E, J=J, sap=sap: E.dma_start(
                    out=toep[:, J, :], in_=sap), "qp")
            self.wait("pe", "qp")
            for J in range(2):
                self.emit("pe", lambda E, J=J: E.matmul(
                    pb[:, J * 256:(J + 1) * 256], rev_b[:], toep[:, J, :],
                    start=True, stop=True), inc="pe")
            biasT_t = self.sb("biasT_t", [128, 2, 256], BF16, st)
            self.wait("dve", "pe")
            self.emit("dve", lambda E, biasT_t=biasT_t: E.tensor_tensor(
                biasT_t[:], pb[:].rearrange("p (j i) -> p j i", j=2),
                gmask_f[:], ALU.add), inc="dve")
            self.wait("sync", "dve")
            self.dma("sync", lambda E, biasT_t=biasT_t: E.dma_start(
                out=d["biasT_d"], in_=biasT_t[:]), "qs")
            self.barrier()

    def finish(self):
        nc = self.nc
        with nc.Block() as block:
            def runner(entries):
                def run(E):
                    for kind, a, b, c in entries:
                        if kind == "op":
                            ins = a(E)
                            if b is not None:
                                ins.then_inc(b, c)
                        else:
                            E.wait_ge(a, b)
                return run
            block.tensor(runner(self.entries["pe"]))
            block.vector(runner(self.entries["dve"]))
            block.scalar(runner(self.entries["act"]))
            block.gpsimd(runner(self.entries["pool"]))
            block.sync(runner(self.entries["sync"]))
        return nc


def build(seq, phases=("A", "B", "C", "D", "E"), scratch_out=()):
    with contextlib.ExitStack() as stack:
        mk = MK(seq, phases, scratch_out, stack)
        mk.preamble()
        for ph in phases:
            getattr(mk, "phase" + ph)()
            mk.barrier()
        nc = mk.finish()
    return nc


def kernel(**inputs):
    sh, per_core = host_prep(inputs)
    seq = inputs["L0_attn_x"].shape[1]
    nc = build(seq)
    in_maps = []
    for pc in per_core:
        m = dict(sh)
        m.update(pc)
        in_maps.append(m)
    res = run_bass_kernel_spmd(nc, in_maps, core_ids=list(range(len(per_core))))
    outs = []
    for r in res.results:
        o = np.asarray(r["out"])
        o = o.transpose(2, 1, 0, 3).reshape(D, seq)
        outs.append(o.T)
    return np.stack(outs, 0).astype(np.float32)


def attach_phases(MK):
    LN = dict(mean=0, msq=1, m2=2, var=3, lv=4, A=5, B=6, ve=7, rec=8, u=9,
              lu=10, R2=11)

    def load_weight(self, dst, src_name, nk):
        src = self.d[src_name]
        for kc in range(nk):
            self.dma("pool", lambda E, kc=kc, dst=dst, src=src: E.dma_start(
                out=dst[:, kc, :], in_=src[kc * 128:(kc + 1) * 128, :]), "qp")

    def ln_tile(self, psS, xf, double=False, qs_sem="qs", qs_val=None,
                eng="dve", pe_guard=None, half_vals=None):
        T = TILE
        r1 = self.row1
        KH = KC // 2

        def sq_emit(dst, src):
            if eng == "pool":
                self.emit("pool", lambda E, dst=dst, src=src: E.tensor_tensor(
                    dst, src, src, ALU.mult), inc="pool")
                return ("pool", self.cnt["pool"])
            self.wait("act", eng)
            self.emit("act", lambda E, dst=dst, src=src: E.activation(
                dst, src, AF.Square), inc="act")
            return ("act", self.cnt["act"])

        if half_vals is None:
            halves = [(qs_sem, qs_val, slice(0, KC))]
        else:
            halves = [("qh" + qs_sem[-1], half_vals[0], slice(0, KH)),
                      (qs_sem, half_vals[1], slice(KH, KC))]
        ev_cp = []
        ev_sq = []
        for i, (sem, val, h) in enumerate(halves):
            self.wait(eng, sem, val)
            if i == 0:
                if qs_sem != "qs":
                    self.wait(eng, "qs")
                self.wait(eng, "pe", pe_guard)
            self.emit(eng, lambda E, h=h: E.tensor_copy(
                self.xbf_g[:, h, :], xf[:, h, :]), inc=eng)
            ev_cp.append((eng, self.cnt[eng]))
            ev_sq.append(sq_emit(self.sqb_g[:, h, :], self.xbf_g[:, h, :]))
        self.wait("pe", *ev_cp[0])
        for kc in range(KC):
            if half_vals is not None and kc == KH:
                self.wait("pe", *ev_cp[1])
            self.emit("pe", lambda E, kc=kc: E.matmul(
                psS[0:1, 0:T], self.ones1_b[:, 0:1], self.xbf_g[:, kc, :],
                start=(kc == 0), stop=(kc == KC - 1)))
        self.wait("pe", *ev_sq[-1])
        for kc in range(KC):
            self.emit("pe", lambda E, kc=kc: E.matmul(
                psS[0:1, T:2 * T], self.ones1_b[:, 0:1], self.sqb_g[:, kc, :],
                start=(kc == 0), stop=(kc == KC - 1)),
                inc="pe" if kc == KC - 1 else None)
        ev_stats = self.cnt["pe"]
        self._ln_pe = ev_stats
        self.wait("act", "pe", ev_stats)
        self.emit("act", lambda E: E.activation(r1(LN["mean"]), psS[0:1, 0:T],
                                                AF.Copy, scale=1.0 / D))
        self.emit("act", lambda E: E.activation(r1(LN["msq"]), psS[0:1, T:2 * T],
                                                AF.Copy, scale=1.0 / D), inc="act")
        self._ln_pstat_ev = self.cnt["act"]
        self.wait("dve", "act")
        self.emit("dve", lambda E: E.tensor_tensor(r1(LN["m2"]), r1(LN["mean"]),
                                                   r1(LN["mean"]), ALU.mult))
        self.emit("dve", lambda E: E.tensor_tensor(r1(LN["var"]), r1(LN["msq"]),
                                                   r1(LN["m2"]), ALU.subtract), inc="dve")
        self.wait("act", "dve")
        self.emit("act", lambda E: E.activation(r1(LN["lv"]), r1(LN["var"]), AF.Ln,
                                                bias=self.epsr[0:1, 0:1]))
        self.emit("act", lambda E: E.activation(r1(LN["A"]), r1(LN["lv"]), AF.Exp,
                                                scale=-0.5), inc="act")
        self.wait("dve", "act")
        self.emit("dve", lambda E: E.tensor_tensor(r1(LN["B"]), r1(LN["mean"]),
                                                   r1(LN["A"]), ALU.mult), inc="dve")
        self._ln_evAB = (self.cnt["dve"], self.cnt["act"])
        if double:
            self.emit("dve", lambda E: E.tensor_scalar(
                r1(LN["ve"]), r1(LN["var"]), self.epsr[0:1, 0:1], None, ALU.add))
            self.emit("dve", lambda E: E.reciprocal(r1(LN["rec"]), r1(LN["ve"])))
            self.emit("dve", lambda E: E.tensor_tensor(r1(LN["u"]), r1(LN["var"]),
                                                       r1(LN["rec"]), ALU.mult), inc="dve")
            self.wait("act", "dve")
            self.emit("act", lambda E: E.activation(r1(LN["lu"]), r1(LN["u"]), AF.Ln,
                                                    bias=self.epsr[0:1, 0:1]))
            self.emit("act", lambda E: E.activation(r1(LN["R2"]), r1(LN["lu"]), AF.Exp,
                                                    scale=-0.5), inc="act")
        self._ln_ev = (self.cnt["dve"], self.cnt["act"])

    def bcast(self, psum, row, rbf_pe=None, ps_act=None, ps_dve=None,
              row_act=None):
        hi = self.rbf[0:1, 0, :]
        lo = self.rbf[0:1, 1, :]
        self.wait("dve", "pe", rbf_pe)
        if row_act is not None:
            self.wait("dve", "act", row_act)
        self.emit("dve", lambda E, row=row, hi=hi: E.tensor_copy(hi, row))
        self.emit("dve", lambda E, row=row, hi=hi, lo=lo: E.tensor_tensor(
            lo, row, hi, ALU.subtract), inc="dve")
        ev_hl = self.cnt["dve"]
        self.wait("pe", "dve", ev_hl if ps_dve is None else max(ev_hl, ps_dve))
        self.wait("pe", "act", ps_act)
        self.emit("pe", lambda E, psum=psum, hi=hi: E.matmul(
            psum[:, 0:TILE], self.ones1r_b[0:1, :], hi, start=True, stop=False))
        self.emit("pe", lambda E, psum=psum, lo=lo: E.matmul(
            psum[:, 0:TILE], self.ones1r_b[0:1, :], lo, start=False, stop=True),
            inc="pe")
        return self.cnt["pe"]

    def phaseA(self):
        d, v = self.d, self.vec
        T = TILE
        with contextlib.ExitStack() as st:
            sb = lambda n, s, dt: self.sb(n, s, dt, st)
            ps = lambda n, s, dt=F32: self.ps(n, s, dt, st)
            wq = sb("A_wq", [128, KC, 1024], BF16)
            wk = sb("A_wk", [128, KC, 1024], BF16)
            wv = sb("A_wv", [128, KC, 1024], BF16)
            wg = sb("A_wg", [128, KC, 1024], BF16)
            wo = sb("A_wo", [128, KC, 1024], BF16)
            for name, dst in (("L0_attn_Wq", wq), ("L0_attn_Wk", wk),
                              ("L0_attn_Wv", wv), ("L0_attn_Wg", wg),
                              ("L0_attn_Wo", wo)):
                load_weight(self, dst, name, KC)
            xfb = [sb(f"A_xf{i}", [128, KC, T], F32) for i in range(2)]
            xn2b = sb("A_xn2b", [128, KC, T], BF16)
            qbb = [sb(f"A_qb{i}", [128, KC, T], BF16) for i in range(2)]
            kbb = [sb(f"A_kb{i}", [128, KC, T], BF16) for i in range(2)]
            qsim = sb("A_qsim", [128, KC, T], BF16)
            ksim = sb("A_ksim", [128, KC, T], BF16)
            lacum = sb("A_lacum", [128, KC, T], F32)
            sqmc = [sb(f"A_sqmc{i}", [128, T], BF16) for i in range(2)]
            alf = sb("A_alf", [128, KC, T], F32)
            cdb = [sb(f"A_cd{i}", [128, KC, CPT], F32) for i in range(2)]
            kdtm = sb("A_kdtm", [128, 1024], BF16)
            v_tm = sb("A_vtm", [128, 2, 1024], BF16)
            oT = sb("A_oT", [128, KC, T], BF16)
            mid = sb("A_mid", [16, CPT], F32)
            midt = sb("A_midt", [16, CPT], F32)
            rqf = sb("A_rqf", [16, T], BF16)
            rkf = sb("A_rkf", [16, T], BF16)
            rkb = sb("A_rkb", [128, KC, T], BF16)
            kvsb = sb("A_kvsb", [128, 512], F32)
            pstat = ps("A_pstat", [16, 512])
            pj = [ps(f"A_pj{i}", [128, 512]) for i in range(2)]
            psq = ps("A_psq", [16, 512])
            psim = ps("A_psim", [128, 512])
            pso = ps("A_pso", [128, 512])
            pskv = ps("A_pskv", [128, 512])
            pskt = ps("A_pskt", [64, 1024], BF16)
            bq2 = sb("A_bq2", [128, KC], F32)
            bk2 = sb("A_bk2", [128, KC], F32)
            bg2 = sb("A_bg2", [128, KC], F32)
            bvr = sb("A_bvr", [1, 1024], BF16)
            lnbb = sb("A_lnbb", [128, KC], BF16)
            self.barrier()
            ev_load = {0: self.dma("sync", lambda E: E.dma_start(
                out=xfb[0][:], in_=d["x_in"][0]), "ql0")}
            self.emit("dve", lambda E: E.tensor_copy(lnbb[:], v["ln1b"][:]),
                      inc="dve")
            self.wait("pe", "dve")
            self.wait("pe", "qp")
            for wi, w8 in enumerate((wq, wk, wg)):
                pp = pj[wi % 2]
                self.wait("pe", "dve")
                for mc in range(KC):
                    for kc in range(KC):
                        self.emit("pe", lambda E, w8=w8, mc=mc, kc=kc, pp=pp:
                                  E.matmul(
                            pp[:, mc:mc + 1], w8[:, kc, mc * 128:(mc + 1) * 128],
                            lnbb[:, kc:kc + 1], start=(kc == 0),
                            stop=(kc == KC - 1)),
                            inc="pe" if (mc == KC - 1 and kc == KC - 1) else None)
                evb0 = self.cnt["pe"]
                self.wait("dve", "pe", evb0)
                dst = (bq2, bk2, bg2)[wi]
                if wi == 2:
                    self.emit("dve", lambda E, dst=dst, pp=pp: E.tensor_tensor(
                        dst[:], pp[:, 0:KC], v["bg"][:], ALU.add), inc="dve")
                else:
                    self.emit("dve", lambda E, dst=dst, pp=pp: E.tensor_copy(
                        dst[:], pp[:, 0:KC]), inc="dve")
            for ns in range(2):
                self.wait("pe", "dve")
                for kc in range(KC):
                    self.emit("pe", lambda E, ns=ns, kc=kc: E.matmul(
                        pso[0:1, 0:512], lnbb[:, kc:kc + 1],
                        wv[:, kc, ns * 512:(ns + 1) * 512],
                        start=(kc == 0), stop=(kc == KC - 1)),
                        inc="pe")
                self.wait("dve", "pe")
                self.emit("dve", lambda E, ns=ns: E.tensor_copy(
                    bvr[:, ns * 512:(ns + 1) * 512], pso[0:1, 0:512]), inc="dve")
            self.wait("pe", "dve")
            L = locals()
            EV = {}
            ln_tile(self, L["pstat"], xfb[0], double=True, qs_sem="ql0",
                    qs_val=ev_load[0])
            for t in range(self.nt):
                _A_stage1a(self, t, L, ev_load, EV)
                _A_stage1b(self, t, L, ev_load, EV)
                if t >= 1:
                    _A_stage2(self, t - 1, L, EV)
                if t + 1 < self.nt:
                    h1, h2 = slice(0, KC // 2), slice(KC // 2, KC)
                    if t >= 1:
                        self.wait("sync", "qw%d" % ((t - 1) % 2),
                                  EV[("woq", t - 1, 0)])
                    ev_l1 = self.dma("sync", lambda E, t=t, h1=h1: E.dma_start(
                        out=xfb[(t + 1) % 2][:, h1, :],
                        in_=d["x_in"][t + 1][:, h1, :]), "qh%d" % ((t + 1) % 2))
                    self.wait("sync", "qs", EV.get(("woq", t - 1, KC // 2)))
                    ev_load[t + 1] = self.dma("sync", lambda E, t=t, h2=h2:
                                              E.dma_start(
                        out=xfb[(t + 1) % 2][:, h2, :],
                        in_=d["x_in"][t + 1][:, h2, :]), "ql%d" % ((t + 1) % 2))
                _A_chain(self, t, L, EV)
                if t + 1 < self.nt:
                    self.wait("dve", "act")
                    self.wait("dve", "pool", EV[("fact", t)])
                    self.wait("act", "pool", EV[("fact", t)])
                    ln_tile(self, pstat, xfb[(t + 1) % 2], double=True,
                            qs_sem="ql%d" % ((t + 1) % 2), qs_val=ev_load[t + 1],
                            half_vals=(ev_l1, ev_load[t + 1]))
                    EV[("lnact", t + 1)] = self._ln_pstat_ev
                    EV[("lnAB", t + 1)] = self._ln_evAB[1]
                    EV[("lnrows", t + 1)] = self._ln_ev[1]
            _A_stage2(self, self.nt - 1, L, EV)

    def _A_stage1a(self, t, L, ev_load, EV):
        d, v = self.d, self.vec
        T = TILE
        r1 = self.row1
        xn2b, qb, kb, lacum, sqmc, alf = (L["xn2b"], L["qbb"][t % 2],
                                          L["kbb"][t % 2], L["lacum"],
                                          L["sqmc"], L["alf"])
        xf = L["xfb"][t % 2]
        pstat, pj, psq, pso = L["pstat"], L["pj"], L["psq"], L["pso"]
        wq, wk, wg = L["wq"], L["wk"], L["wg"]
        tmpf = self.tmpf_g

        zmm = EV.get(("zmm", t - 1))
        cpz = EV.get(("cpz", t - 1))
        apl = EV.get(("wo", t - 2), EV.get(("apl", t - 1)))
        lnab = EV.get(("lnAB", t))
        lnr = EV.get(("lnrows", t))
        evA = bcast(self, pj[0], r1(LN["A"]), rbf_pe=zmm, ps_act=cpz, ps_dve=apl,
                    row_act=lnab)
        evB = bcast(self, pj[1], r1(LN["B"]), rbf_pe=evA, ps_act=cpz, ps_dve=apl,
                    row_act=lnab)
        evR = bcast(self, pso, r1(LN["R2"]), rbf_pe=evB, ps_act=cpz, ps_dve=apl,
                    row_act=lnr)
        self.wait("dve", "pe", evB)
        bA = pj[0][:, 0:T].unsqueeze(1).broadcast_to([128, KC, T])
        bB = pj[1][:, 0:T].unsqueeze(1).broadcast_to([128, KC, T])
        bR = pso[:, 0:T].unsqueeze(1).broadcast_to([128, KC, T])
        if ("zfree", t - 1) in EV:
            self.wait("dve", "pool", EV[("zfree", t - 1)])
        else:
            self.wait("dve", "pool")
        self.emit("dve", lambda E, xf=xf, bA=bA: E.tensor_tensor(
            tmpf[:], xf[:], bA, ALU.mult))
        self.emit("dve", lambda E, bB=bB, xf=xf: E.tensor_tensor(
            xf[:], tmpf[:], bB, ALU.subtract), inc="dve")
        ev_xn = self.cnt["dve"]
        self.wait("sync", "dve", ev_xn)
        self.dma("sync", lambda E, xf=xf, t=t: E.dma_start(
            out=d["xn_s"][t], in_=xf[:]), "qs")
        self.wait("dve", "pe", evR)
        if ("vmm", t - 1) in EV:
            self.wait("dve", "pe", EV[("vmm", t - 1)])
        for kc in range(KC):
            self.emit("dve", lambda E, kc=kc, xf=xf: E.scalar_tensor_tensor(
                xn2b[:, kc, :], xf[:, kc, :], v["ln1w"][:, kc:kc + 1],
                pso[:, 0:T], ALU.mult, ALU.mult),
                inc="dve" if kc == KC - 1 else None)
        ev_xn2 = self.cnt["dve"]
        EV[("apl", t)] = ev_xn2
        EV[("xn2", t)] = ev_xn2

    def _A_stage1b(self, t, L, ev_load, EV):
        d, v = self.d, self.vec
        T = TILE
        xn2b, qb, kb, lacum, sqmc, alf = (L["xn2b"], L["qbb"][t % 2],
                                          L["kbb"][t % 2], L["lacum"],
                                          L["sqmc"], L["alf"])
        pj, psq = L["pj"], L["psq"]
        wq, wk, wg = L["wq"], L["wk"], L["wg"]
        ev_xn2 = EV[("xn2", t)]
        if ("wo", t - 1) in EV:
            self.wait("pe", "dve", EV[("wo", t - 1)])

        def proj_qk(w, dstb, qhalf):
            ev_evac = {}
            off = 0 if qhalf else T

            def ssq_mm(mc):
                self.wait("pe", "act", ev_evac[mc])
                self.emit("pe", lambda E, mc=mc: E.matmul(
                    psq[0:16, off:off + T], self.sq_b[:, mc, :], sqmc[mc % 2][:],
                    start=(mc == 0), stop=(mc == KC - 1)),
                    inc="pe" if mc == KC - 1 else None)

            for mc in range(KC):
                pp = pj[mc % 2]
                if mc >= 2:
                    self.wait("pe", "act", ev_evac[mc - 2])
                self.wait("pe", "dve", ev_xn2)
                for kc in range(KC):
                    self.emit("pe", lambda E, kc=kc, mc=mc, pp=pp, w=w: E.matmul(
                        pp[:, 0:T], w[:, kc, mc * 128:(mc + 1) * 128], xn2b[:, kc, :],
                        start=(kc == 0), stop=(kc == KC - 1)),
                        inc="pe" if kc == KC - 1 else None)
                evmm = self.cnt["pe"]
                self.wait("act", "pe", evmm)
                bcol = L["bq2"] if qhalf else L["bk2"]
                self.emit("act", lambda E, mc=mc, pp=pp, dstb=dstb, bcol=bcol:
                          E.activation(
                    dstb[:, mc, :], pp[:, 0:T], AF.Identity,
                    bias=bcol[:, mc:mc + 1]))
                self.emit("act", lambda E, mc=mc, pp=pp, bcol=bcol: E.activation(
                    sqmc[mc % 2][:], pp[:, 0:T], AF.Square,
                    bias=bcol[:, mc:mc + 1]), inc="act")
                ev_evac[mc] = self.cnt["act"]
                if mc >= 1:
                    ssq_mm(mc - 1)
            ssq_mm(KC - 1)
            return self.cnt["pe"]

        if ("rf", t - 1) in EV:
            self.wait("pe", "dve", EV[("rf", t - 1)])
        EV[("ssq_q", t)] = proj_qk(wq, qb, True)
        EV[("ssq_k", t)] = proj_qk(wk, kb, False)

        ev_sig = {}
        if ("alfp", t - 1) in EV:
            self.wait("act", "pool", EV[("alfp", t - 1)])
        else:
            self.wait("act", "pool")
        for mc in range(KC):
            pp = pj[mc % 2]
            if mc >= 2:
                self.wait("pe", "act", ev_sig[mc - 2])
            self.wait("pe", "dve", ev_xn2)
            for kc in range(KC):
                self.emit("pe", lambda E, kc=kc, mc=mc, pp=pp: E.matmul(
                    pp[:, 0:T], wg[:, kc, mc * 128:(mc + 1) * 128], xn2b[:, kc, :],
                    start=(kc == 0), stop=(kc == KC - 1)),
                    inc="pe" if kc == KC - 1 else None)
            evmm = self.cnt["pe"]
            self.wait("act", "pe", evmm)
            self.emit("act", lambda E, mc=mc, pp=pp: E.activation(
                alf[:, mc, :], pp[:, 0:T], AF.Sigmoid,
                bias=L["bg2"][:, mc:mc + 1]), inc="act")
            ev_sig[mc] = self.cnt["act"]
        EV[("sig", t)] = ev_sig[KC - 1]
        self.wait("dve", "act", ev_sig[KC - 1])
        self.emit("dve", lambda E: E.tensor_scalar(
            alf[:], alf[:], self.c1e6[:, 0:1], None, ALU.max), inc="dve")
        self.wait("act", "dve")
        self.emit("act", lambda E: E.activation(alf[:], alf[:], AF.Ln), inc="act")
        self.wait("dve", "act")
        for mc in range(KC):
            for ch in range(CPT):
                self.emit("dve", lambda E, mc=mc, ch=ch: E.tensor_tensor_scan(
                    lacum[:, mc, ch * CHL:(ch + 1) * CHL],
                    alf[:, mc, ch * CHL:(ch + 1) * CHL], self.zeros64[:],
                    0.0, ALU.add, ALU.add),
                    inc="dve" if (mc == KC - 1 and ch == CPT - 1) else None)
        EV[("lacum", t)] = self.cnt["dve"]

    def _A_chain(self, t, L, EV):
        d, v = self.d, self.vec
        T = TILE
        r16 = self.row16
        (qb, kb, qsim, ksim, lacum, alf, cd, v_tm, mid, midt, rqf, rkf, rkb) = (
            L["qbb"][t % 2], L["kbb"][t % 2], L["qsim"], L["ksim"], L["lacum"],
            L["alf"], L["cdb"][t % 2], L["v_tm"], L["mid"], L["midt"],
            L["rqf"], L["rkf"], L["rkb"])
        pstat, pj, psq = L["pstat"], L["pj"], L["psq"]
        psim, pskv = L["psim"], L["pskv"]
        wv = L["wv"]
        xn2b = L["xn2b"]
        tmpf = self.tmpf_g
        (S_lam, S_z2, S_z, S_eqf, S_ekf, S_ssqm, S_lnr, S_rnq,
         S_rnk) = range(12, 21)
        ev_lacum = EV[("lacum", t)]
        ev_xn2 = EV[("xn2", t)]

        ev_vt = {}
        for i, (m, ns) in enumerate(((0, 0), (0, 1), (1, 0), (1, 1))):
            pp = pj[i % 2]
            if i == 0:
                self.wait("pe", "act", EV[("sig", t)])
                self.wait("pe", "dve", ev_xn2)
                if ("wo", t - 1) in EV:
                    self.wait("pe", "dve", EV[("wo", t - 1)])
            if i >= 2:
                self.wait("pe", "act", ev_vt[i - 2])
            for kc in range(KC):
                self.emit("pe", lambda E, kc=kc, m=m, ns=ns, pp=pp: E.matmul(
                    pp[:, 0:512], xn2b[:, kc, m * 128:(m + 1) * 128],
                    wv[:, kc, ns * 512:(ns + 1) * 512],
                    start=(kc == 0), stop=False))
            self.emit("pe", lambda E, ns=ns, pp=pp: E.matmul(
                pp[:, 0:512], self.ones1r_b[0:1, :],
                L["bvr"][:, ns * 512:(ns + 1) * 512],
                start=False, stop=True), inc="pe")
            self.wait("act", "pe")
            self.emit("act", lambda E, m=m, ns=ns, pp=pp: E.activation(
                v_tm[:, m, ns * 512:(ns + 1) * 512], pp[:, 0:512], AF.Copy),
                inc="act")
            ev_vt[i] = self.cnt["act"]
        ev_vtm = self.cnt["act"]
        EV[("vtm", t)] = ev_vtm
        EV[("vmm", t)] = self.cnt["pe"]

        self.wait("act", "dve", ev_lacum)
        self.emit("act", lambda E, cd=cd: E.activation(
            cd[:], lacum[:, :, CHL - 1::CHL], AF.Exp), inc="act")
        EV[("cd", t)] = self.cnt["act"]

        lacb = L["rkb"]
        self.emit("dve", lambda E, lacb=lacb: E.tensor_copy(lacb[:], lacum[:]),
                  inc="dve")
        self.wait("pe", "dve")
        if ("lnact", t) in EV:
            self.wait("pe", "act", EV[("lnact", t)])
        for kc in range(KC):
            self.emit("pe", lambda E, kc=kc, lacb=lacb: E.matmul(
                pstat[0:16, 0:T], self.sq_b[:, kc, :], lacb[:, kc, :],
                start=(kc == 0), stop=(kc == KC - 1)),
                inc="pe" if kc == KC - 1 else None)
        self.wait("act", "pe")
        self.emit("act", lambda E: E.activation(r16(S_lam), pstat[0:16, 0:T], AF.Copy,
                                                scale=1.0 / DH), inc="act")
        self.wait("dve", "act")
        lam = r16(S_lam)
        self.emit("dve", lambda E: E.tensor_tensor(midt[:], lam[:, 0::CHL],
                                                   lam[:, CHL - 1::CHL], ALU.add))
        self.emit("dve", lambda E: E.tensor_scalar(mid[:], midt[:],
                                                   self.chalf[0:16, 0:1], None, ALU.mult))
        self.emit("dve", lambda E: E.tensor_tensor(
            r16(S_z2).rearrange("p (c k) -> p c k", k=CHL),
            lam.rearrange("p (c k) -> p c k", k=CHL),
            mid[:].unsqueeze(2).broadcast_to([16, CPT, CHL]), ALU.subtract))
        self.emit("dve", lambda E: E.tensor_scalar(
            r16(S_z), r16(S_z2), self.c44[0:16, 0:1], self.cm44[0:16, 0:1],
            ALU.min, ALU.max), inc="dve")
        self.wait("act", "dve")
        self.emit("act", lambda E: E.activation(r16(S_eqf), r16(S_z), AF.Exp))
        self.emit("act", lambda E: E.activation(r16(S_ekf), r16(S_z), AF.Exp,
                                                scale=-1.0), inc="act")
        r2 = lambda sl: self.rowpool[0:16, sl * T:(sl + 2) * T]
        self.wait("dve", "pe", EV[("ssq_k", t)])
        self.wait("dve", "act")
        self.emit("dve", lambda E, r2=r2: E.tensor_scalar(
            r2(S_ssqm), psq[0:16, 0:2 * T], self.c1e24[0:16, 0:1], None,
            ALU.max), inc="dve")
        self.wait("act", "dve")
        self.emit("act", lambda E, r2=r2: E.activation(r2(S_ssqm), r2(S_ssqm),
                                                       AF.Ln))
        self.emit("act", lambda E, r2=r2: E.activation(
            r2(S_rnq), r2(S_ssqm), AF.Exp, scale=-0.5), inc="act")
        self.wait("dve", "act")
        self.emit("dve", lambda E: E.tensor_tensor(rqf[:], r16(S_eqf), r16(S_rnq),
                                                   ALU.mult))
        self.emit("dve", lambda E: E.tensor_tensor(rkf[:], r16(S_ekf), r16(S_rnk),
                                                   ALU.mult), inc="dve")
        EV[("rf", t)] = self.cnt["dve"]

        rqb = self.xbf_g
        zb = tmpf
        zhi = self.rbf[0:16, 0, :]
        zlo = self.rbf[0:16, 1, :]
        self.wait("dve", "pe")
        self.emit("dve", lambda E, zhi=zhi: E.tensor_copy(zhi, r16(S_z)))
        self.emit("dve", lambda E, zhi=zhi, zlo=zlo: E.tensor_tensor(
            zlo, r16(S_z), zhi, ALU.subtract), inc="dve")
        ev_zsplit = self.cnt["dve"]
        ev_zc = {}
        for cyc in range(KC // 2):
            pz = psim if cyc % 2 == 0 else pskv
            if cyc >= 2:
                self.wait("pe", "dve", ev_zc[cyc - 2])
            elif cyc == 0:
                self.wait("pe", "dve", ev_zsplit)
                if ("attnc", t - 1) in EV:
                    self.wait("pe", "act", EV[("attnc", t - 1)])
            for sub2 in range(2):
                mc = 2 * cyc + sub2
                off = sub2 * T
                self.emit("pe", lambda E, mc=mc, pz=pz, off=off, zhi=zhi: E.matmul(
                    pz[:, off:off + T], self.hm_b[:, mc, :], zhi, start=True,
                    stop=False))
                self.emit("pe", lambda E, mc=mc, pz=pz, off=off, zlo=zlo: E.matmul(
                    pz[:, off:off + T], self.hm_b[:, mc, :], zlo, start=False,
                    stop=True), inc="pe" if sub2 == 1 else None)
            evz = self.cnt["pe"]
            self.wait("dve", "pe", evz)
            self.emit("dve", lambda E, cyc=cyc, pz=pz: E.tensor_copy(
                zb[:, 2 * cyc:2 * cyc + 2, :],
                pz[:, 0:2 * T].rearrange("p (a n) -> p a n", a=2)), inc="dve")
            ev_zc[cyc] = self.cnt["dve"]
        ev_zbd = self.cnt["dve"]
        EV[("zbd", t)] = ev_zbd
        ev_cp = {}
        for mc in range(KC):
            pqk = pj[mc % 2]
            if mc >= 2:
                self.wait("pe", "act", ev_cp[mc - 2])
            else:
                self.wait("pe", "act", ev_vtm)
                if ("wo", t - 1) in EV:
                    self.wait("pe", "dve", EV[("wo", t - 1)])
            self.emit("pe", lambda E, mc=mc, pqk=pqk: E.matmul(
                pqk[:, 0:T], self.hm_b[:, mc, :], rqf[:], start=True, stop=True))
            self.emit("pe", lambda E, mc=mc, pqk=pqk: E.matmul(
                pqk[:, T:2 * T], self.hm_b[:, mc, :], rkf[:], start=True, stop=True),
                inc="pe")
            evb = self.cnt["pe"]
            self.wait("act", "pe", evb)
            self.emit("act", lambda E, mc=mc, pqk=pqk: E.activation(
                rqb[:, mc, :], pqk[:, 0:T], AF.Copy))
            self.emit("act", lambda E, mc=mc, pqk=pqk: E.activation(
                rkb[:, mc, :], pqk[:, T:2 * T], AF.Copy), inc="act")
            ev_cp[mc] = self.cnt["act"]
        EV[("zmm", t)] = self.cnt["pe"]
        EV[("cpz", t)] = ev_cp[KC - 1]
        eqb = self.sqb_g
        self.wait("pool", "dve", ev_zbd)
        self.emit("pool", lambda E: E.tensor_tensor(alf[:], lacum[:], zb[:],
                                                    ALU.subtract), inc="pool")
        self.wait("act", "pool")
        self.emit("act", lambda E, eqb=eqb: E.activation(eqb[:], alf[:], AF.Exp),
                  inc="act")
        ev_eq = self.cnt["act"]
        EV[("zfree", t)] = self.cnt["pool"]
        self.wait("pool", "act", ev_eq)
        for mc in range(KC):
            lastb = lacum[:, mc, CHL - 1::CHL].unsqueeze(2).broadcast_to(
                [128, CPT, CHL])
            self.emit("pool", lambda E, mc=mc, lastb=lastb: E.tensor_tensor(
                alf[:, mc, :].rearrange("p (c l) -> p c l", l=CHL), lastb,
                alf[:, mc, :].rearrange("p (c l) -> p c l", l=CHL), ALU.subtract),
                inc="pool" if mc == KC - 1 else None)
        EV[("alfp", t)] = self.cnt["pool"]
        self.wait("pool", "act", ev_cp[KC - 1])
        self.emit("pool", lambda E, qb=qb: E.tensor_tensor(qsim[:], qb[:], rqb[:],
                                                           ALU.mult))
        self.emit("pool", lambda E, kb=kb: E.tensor_tensor(ksim[:], kb[:], rkb[:],
                                                           ALU.mult), inc="pool")
        EV[("simin", t)] = self.cnt["pool"]
        self.emit("pool", lambda E, qb=qb, eqb=eqb: E.tensor_tensor(
            qb[:], qsim[:], eqb[:], ALU.mult), inc="pool")
        ev_qd = self.cnt["pool"]
        self.wait("act", "pool", ev_qd)
        self.emit("act", lambda E: E.activation(rqb[:], alf[:], AF.Exp), inc="act")
        self.wait("pool", "act")
        self.emit("pool", lambda E, kb=kb: E.tensor_tensor(kb[:], ksim[:], rqb[:],
                                                           ALU.mult), inc="pool")
        EV[("fact", t)] = self.cnt["pool"]

    def _A_stage2(self, t, L, EV):
        d = self.d
        T = TILE
        (qsim, ksim, cd, kdtm, v_tm, oT, kvsb) = (
            L["qsim"], L["ksim"], L["cdb"][t % 2], L["kdtm"],
            L["v_tm"], L["oT"], L["kvsb"])
        attnT = L["rkb"][:].rearrange("p c t -> p (c t)")
        qdec, kdec = L["qbb"][t % 2], L["kbb"][t % 2]
        xf = L["xfb"][t % 2]
        pj = L["pj"]
        psim, pso, pskv, pskt = L["psim"], L["pso"], L["pskv"], L["pskt"]
        wo = L["wo"]

        rowstate = {0: None, 64: None}

        def mmq(row, col, make, inc=None):
            if rowstate[col] is not None and rowstate[col] != row:
                self.emit("pe", lambda E: E.ldweights(self.ident_b[:]))
                rowstate[0] = rowstate[64] = None
            rowstate[col] = row
            return self.emit("pe", make, inc=inc)

        def full_k():
            rowstate[0] = rowstate[64] = None

        ST = {}

        def gla_head(ch):
            par = (ch % 2) * 64
            tok = slice(ch * CHL, (ch + 1) * CHL)
            st = ST[ch] = {}
            if ch == 0:
                self.wait("pe", "pool", EV[("simin", t)])
                self.wait("pe", "dve", EV[("zbd", t)])
                if ("attnc", t - 1) in EV:
                    self.wait("pe", "act", EV[("attnc", t - 1)])
            else:
                self.wait("pe", "act", ST[ch - 1]["attc"])
            ev_attc = None
            for hg in range(2):
                if hg == 1:
                    self.wait("pe", "act", ev_attc)
                hhs = [0, 2, 4, 6, 1, 3, 5, 7]
                for hh in hhs:
                    h = hg * 8 + hh
                    fb = (h % 2) * 64
                    kcq = h // 2
                    mmq(fb, par, lambda E, h=h, hh=hh, fb=fb, kcq=kcq, par=par,
                        tok=tok: E.matmul(
                        psim[par:par + 64, hh * 64:(hh + 1) * 64],
                        ksim[fb:fb + 64, kcq, tok], qsim[fb:fb + 64, kcq, tok],
                        start=(hh == 0), stop=(hh == 7), tile_position=(fb, par)),
                        inc="pe" if hh == 7 else None)
                ev_simg = self.cnt["pe"]
                if hg == 0:
                    if ch == 0:
                        self.wait("pe", "pool", EV[("fact", t)])
                        if ("kdtm", t - 1) in EV:
                            self.wait("pe", "act", EV[("kdtm", t - 1)])
                    else:
                        self.wait("pe", "act", ST[ch - 1]["kdtm"])
                    for kc in range(KC):
                        self.emit("pe", lambda E, kc=kc, tok=tok: E.transpose(
                            pskt[0:64, kc * 128:(kc + 1) * 128], kdec[:, kc, tok],
                            self.ident_b[:]), inc="pe" if kc == KC - 1 else None)
                    full_k()
                    st["kt"] = self.cnt["pe"]
                self.wait("act", "pe", ev_simg)
                self.emit("act", lambda E, par=par, hg=hg: E.copy(
                    attnT[par:par + 64, hg * 512:(hg + 1) * 512],
                    psim[par:par + 64, :]), inc="act")
                ev_attc = self.cnt["act"]
                self.wait("pool", "act", ev_attc)
                self.emit("pool", lambda E, par=par, hg=hg: E.tensor_tensor(
                    attnT[par:par + 64, hg * 512:(hg + 1) * 512].rearrange(
                        "p (h c) -> p h c", h=8),
                    attnT[par:par + 64, hg * 512:(hg + 1) * 512].rearrange(
                        "p (h c) -> p h c", h=8),
                    self.triu_b[par:par + 64, :].unsqueeze(1).broadcast_to(
                        [64, 8, 64]), ALU.mult), inc="pool")
            st["attc"] = ev_attc
            st["attn"] = self.cnt["pool"]
            self.wait("act", "pe", st["kt"])
            self.emit("act", lambda E, par=par: E.copy(kdtm[par:par + 64, :],
                                                       pskt[0:64, :]), inc="act")
            st["kdtm"] = self.cnt["act"]

        def gla_tail(ch):
            par = (ch % 2) * 64
            sub = ch // 2
            tok = slice(ch * CHL, (ch + 1) * CHL)
            st = ST[ch]
            if ch >= 1:
                self.wait("pe", "pool", ST[ch - 1]["supd"])
            else:
                self.wait("pe", "pool", st["attn"])
            self.wait("pe", "act", EV[("vtm", t)])
            if ch == 0:
                if ("apl", t + 1) in EV:
                    self.wait("pe", "dve", EV[("apl", t + 1)])
                else:
                    self.wait("pe", "dve")
                if ("oTev", t - 1) in EV:
                    self.wait("pe", "act", EV[("oTev", t - 1)])

            else:
                self.wait("pe", "act", ST[ch - 1]["oTe"])
            for parity in range(2):
                fb = parity * 64
                for pair in range(KC):
                    mmq(fb, fb, lambda E, fb=fb, pair=pair, tok=tok: E.matmul(
                        pso[fb:fb + 64, pair * 64:(pair + 1) * 64],
                        self.S_b[fb:fb + 64, pair * 64:(pair + 1) * 64],
                        qdec[fb:fb + 64, pair, tok],
                        start=(pair == 0), stop=False, tile_position=(fb, fb)))
                for pair in range(KC):
                    h = 2 * pair + parity
                    mmq(par, fb, lambda E, h=h, fb=fb, pair=pair, par=par, sub=sub:
                        E.matmul(
                        pso[fb:fb + 64, pair * 64:(pair + 1) * 64],
                        v_tm[par:par + 64, sub, h * 64:(h + 1) * 64],
                        attnT[par:par + 64, h * 64:(h + 1) * 64],
                        start=False, stop=(pair == KC - 1), tile_position=(par, fb)),
                        inc="pe" if (parity == 1 and pair == KC - 1) else None)
            ev_omm = self.cnt["pe"]
            self.wait("pe", "act", st["kdtm"])
            if ch >= 1:
                self.wait("pe", "act", ST[ch - 1]["kvc"])
            for parity in range(2):
                fb = parity * 64
                for pair in range(KC):
                    h = 2 * pair + parity
                    mmq(par, fb, lambda E, h=h, fb=fb, pair=pair, par=par, sub=sub:
                        E.matmul(
                        pskv[fb:fb + 64, pair * 64:(pair + 1) * 64],
                        kdtm[par:par + 64, pair * 128 + fb:pair * 128 + fb + 64],
                        v_tm[par:par + 64, sub, h * 64:(h + 1) * 64],
                        start=(pair == 0), stop=(pair == KC - 1),
                        tile_position=(par, fb)),
                        inc="pe" if (parity == 1 and pair == KC - 1) else None)
            ev_kv = self.cnt["pe"]
            self.wait("act", "pe", ev_omm)
            self.emit("act", lambda E, tok=tok: E.copy(
                oT[:, :, tok], pso[:].rearrange("p (k e) -> p k e", e=64)), inc="act")
            st["oTe"] = self.cnt["act"]
            self.wait("act", "pe", ev_kv)
            self.emit("act", lambda E: E.copy(kvsb[:], pskv[:]), inc="act")
            st["kvc"] = self.cnt["act"]
            self.wait("pool", "act", st["kvc"])
            if ch == 0:
                self.wait("pool", "act", EV[("cd", t)])
            cdap = cd[:, :, ch].unsqueeze(2).broadcast_to([128, KC, CHL])
            self.emit("pool", lambda E, cdap=cdap: E.tensor_tensor(
                self.S_f[:].rearrange("p (k e) -> p k e", e=64),
                self.S_f[:].rearrange("p (k e) -> p k e", e=64), cdap, ALU.mult))
            self.emit("pool", lambda E, kvsb=kvsb: E.tensor_tensor(
                self.S_f[:], self.S_f[:], kvsb[:], ALU.add))
            self.emit("pool", lambda E: E.tensor_copy(self.S_b[:], self.S_f[:]),
                      inc="pool")
            st["supd"] = self.cnt["pool"]

        for ch in range(CPT):
            gla_head(ch)
            if ch >= 1:
                gla_tail(ch - 1)
        gla_tail(CPT - 1)
        ev_kdtm = ST[CPT - 1]["kdtm"]
        ev_oTe = ST[CPT - 1]["oTe"]
        ev_kvc = ST[CPT - 1]["kvc"]
        EV[("attn", t)] = self.cnt["pool"]
        EV[("attnc", t)] = ev_kvc
        EV[("kdtm", t)] = ev_kdtm
        EV[("oTev", t)] = ev_oTe
        ev_oT = self.cnt["act"]

        ev_wo = {}
        self.wait("dve", "qs")
        if ("apl", t + 1) in EV:
            self.wait("pe", "dve", EV[("apl", t + 1)])
        else:
            self.wait("pe", "dve")
        for mc in range(KC):
            pp = pj[mc % 2]
            self.wait("pe", "act", ev_oT)
            if mc >= 2:
                self.wait("pe", "dve", ev_wo[mc - 2])
            for kc in range(KC):
                self.emit("pe", lambda E, kc=kc, mc=mc, pp=pp: E.matmul(
                    pp[:, 0:T], wo[:, kc, mc * 128:(mc + 1) * 128], oT[:, kc, :],
                    start=(kc == 0), stop=(kc == KC - 1)),
                    inc="pe" if kc == KC - 1 else None)
            self.wait("dve", "pe")
            self.emit("dve", lambda E, mc=mc, pp=pp, xf=xf: E.tensor_tensor(
                xf[:, mc, :], pp[:, 0:T], xf[:, mc, :], ALU.add), inc="dve")
            ev_wo[mc] = self.cnt["dve"]
            if mc == KC // 2 - 1 or mc == KC - 1:
                h = slice(0, KC // 2) if mc < KC - 1 else slice(KC // 2, KC)
                q = "qw%d" % (t % 2) if mc < KC - 1 else "qs"
                self.wait("sync", "dve", ev_wo[mc])
                self.dma("sync", lambda E, xf=xf, t=t, h=h: E.dma_start(
                    out=d["xa_s"][t][:, h, :], in_=xf[:, h, :]), q)
                EV[("woq", t, h.start)] = self.cnt[q]
        EV[("wo", t)] = self.cnt["dve"]

    def phaseB(self):
        self._b2st = {}
        d, v = self.d, self.vec
        T = TILE
        T2 = 2 * T
        nt2 = self.nt // 2
        with contextlib.ExitStack() as st:
            sb = lambda n, s_, dt: self.sb(n, s_, dt, st)
            ps = lambda n, s_, dt=F32: self.ps(n, s_, dt, st)
            w1 = sb("B_w1", [128, KC, FFN], BF16)
            load_weight(self, w1, "L0_attn_W1", KC)
            xfb = [sb(f"B_xf{i}", [128, KC, T], F32) for i in range(2)]
            hb = sb("B_hb", [128, KC, T2], BF16)
            u5 = sb("B_u5", [128, 32, T2], BF16)
            b1f = sb("B_b1f", [128, 32], F32)
            lnbb = sb("B_lnbb", [128, KC], BF16)
            pstat = ps("B_pstat", [16, 512])
            pj = [ps(f"B_pj{i}", [128, 512]) for i in range(3)]
            self.barrier()
            tmpf = self.tmpf_g
            r1 = self.row1
            ev_load = {0: self.dma("sync", lambda E: E.dma_start(
                out=xfb[0][:], in_=d["xa_s"][0]), "ql0")}
            self.emit("dve", lambda E: E.tensor_copy(lnbb[:], v["ln2b"][:]),
                      inc="dve")
            self.wait("pe", "dve")
            self.wait("pe", "qp")
            pjb = pj[2]
            for mh in range(32):
                for kc in range(KC):
                    self.emit("pe", lambda E, mh=mh, kc=kc, pjb=pjb: E.matmul(
                        pjb[:, mh:mh + 1], w1[:, kc, mh * 128:(mh + 1) * 128],
                        lnbb[:, kc:kc + 1], start=(kc == 0), stop=(kc == KC - 1)),
                        inc="pe" if (mh == 31 and kc == KC - 1) else None)
            self.wait("dve", "pe")
            self.emit("dve", lambda E, pjb=pjb: E.tensor_tensor(
                b1f[:], pjb[:, 0:32], v["b1"][:], ALU.add), inc="dve")
            self.wait("pe", "dve")
            ln_tile(self, pstat, xfb[0], double=False, qs_sem="ql0",
                    qs_val=ev_load[0])
            for t in range(self.nt):
                xf = xfb[t % 2]
                half = (t % 2) * T
                if t + 1 < self.nt:
                    self.wait("sync", "dve")
                    self.wait("sync", "qs")
                    ev_load[t + 1] = self.dma("sync", lambda E, t=t: E.dma_start(
                        out=xfb[(t + 1) % 2][:], in_=d["xa_s"][t + 1]),
                        "ql%d" % ((t + 1) % 2))
                evA = bcast(self, pj[0], r1(LN["A"]))
                evB = bcast(self, pj[1], r1(LN["B"]))
                self.wait("dve", "pe", evB)
                self.wait("dve", "pool")
                bA = pj[0][:, 0:T].unsqueeze(1).broadcast_to([128, KC, T])
                bB = pj[1][:, 0:T].unsqueeze(1).broadcast_to([128, KC, T])
                self.emit("dve", lambda E, xf=xf, bA=bA: E.tensor_tensor(
                    tmpf[:], xf[:], bA, ALU.mult))
                self.emit("dve", lambda E, bB=bB: E.tensor_tensor(
                    tmpf[:], tmpf[:], bB, ALU.subtract), inc="dve")
                self.wait("dve", "pe")
                hbh = hb[:].rearrange("p c (h n) -> p c h n", n=T)[
                    :, :, half // T, :]
                for kc in range(KC):
                    self.emit("dve", lambda E, kc=kc, hbh=hbh: E.tensor_scalar(
                        hbh[:, kc, :], tmpf[:, kc, :], v["ln2w"][:, kc:kc + 1],
                        None, ALU.mult), inc="dve" if kc == KC - 1 else None)
                ev_h = self.cnt["dve"]
                if t + 1 < self.nt:
                    ln_tile(self, pstat, xfb[(t + 1) % 2], double=False,
                            qs_sem="ql%d" % ((t + 1) % 2), qs_val=ev_load[t + 1])
                if t % 2 == 1:
                    ev_gelu = {}
                    for mh in range(32):
                        pp = pj[mh % 3]
                        if mh >= 3:
                            self.wait("pe", "act", ev_gelu[mh - 3])
                        self.wait("pe", "dve", ev_h)
                        if mh == 0:
                            self.wait("pe", "qs")
                        for kc in range(KC):
                            self.emit("pe", lambda E, kc=kc, mh=mh, pp=pp: E.matmul(
                                pp[:], w1[:, kc, mh * 128:(mh + 1) * 128],
                                hb[:, kc, :], start=(kc == 0), stop=(kc == KC - 1)),
                                inc="pe" if kc == KC - 1 else None)
                        evmm = self.cnt["pe"]
                        self.wait("act", "pe", evmm)
                        self.emit("act", lambda E, mh=mh, pp=pp: E.activation(
                            u5[:, mh, :], pp[:], AF.Gelu,
                            bias=b1f[:, mh:mh + 1]), inc="act")
                        ev_gelu[mh] = self.cnt["act"]
                    self.wait("sync", "act", ev_gelu[31])
                    self.dma("sync", lambda E, t=t: E.dma_start(
                        out=d["u_s"][t // 2], in_=u5[:]), "qs")
        self.barrier()
        with contextlib.ExitStack() as st:
            sb = lambda n, s_, dt: self.sb(n, s_, dt, st)
            ps = lambda n, s_, dt=F32: self.ps(n, s_, dt, st)
            w2 = sb("B2_w2", [128, 32, 1024], BF16)
            load_weight(self, w2, "L0_attn_W2", 32)
            u5b = [sb(f"B2_u{i}", [128, 32, T2], BF16) for i in range(2)]
            xf5 = sb("B2_xf", [128, KC, T2], F32)
            t15 = [sb(f"B2_t1{i}", [128, T2], F32) for i in range(2)]
            pj = [ps(f"B2_pj{i}", [128, 512]) for i in range(2)]
            self.barrier()

            def xa_pair(stt):
                return d["xa_s"][2 * stt:2 * stt + 2].rearrange(
                    "t p c n -> p c t n")

            def v4(buf):
                return buf[:].rearrange("p c (t n) -> p c t n", t=2)

            def s_half(stt, mc):
                return d["s_s"][2 * stt:2 * stt + 2, :, mc, :].rearrange(
                    "t p n -> p t n")

            ev_u = {0: self.dma("sync", lambda E: E.dma_start(
                out=u5b[0][:], in_=d["u_s"][0]), "ql0")}
            ev_x = {0: self.dma("sync", lambda E: E.dma_start(
                out=v4(xf5), in_=xa_pair(0)), "qs")}
            for stt in range(nt2):
                u5c = u5b[stt % 2]
                if stt + 1 < nt2:
                    self.wait("sync", "pe")
                    ev_u[stt + 1] = self.dma("sync", lambda E, stt=stt: E.dma_start(
                        out=u5b[(stt + 1) % 2][:], in_=d["u_s"][stt + 1]),
                        "ql%d" % ((stt + 1) % 2))
                self.wait("pe", "ql%d" % (stt % 2), ev_u[stt])
                if stt >= 1:
                    self.wait("pe", "dve", prev_last)
                ev_ev = {}
                for mc in range(KC):
                    pp = pj[mc % 2]
                    if mc >= 2:
                        self.wait("pe", "dve", ev_ev[mc - 2])
                    for mh in range(32):
                        self.emit("pe", lambda E, mh=mh, mc=mc, pp=pp, u5c=u5c:
                                  E.matmul(
                            pp[:], w2[:, mh, mc * 128:(mc + 1) * 128], u5c[:, mh, :],
                            start=(mh == 0), stop=(mh == 31)),
                            inc="pe" if mh == 31 else None)
                    evmm = self.cnt["pe"]
                    self.wait("dve", "pe", evmm)
                    if mc == 0:
                        self.wait("dve", "qs")
                    ev_st = getattr(self, "_b2st", {})
                    if (stt * KC + mc) >= 2:
                        self.wait("dve", "qt%d" % (mc % 2),
                                  ev_st[(stt * KC + mc - 2)])
                    self.emit("dve", lambda E, mc=mc, pp=pp: E.tensor_scalar(
                        t15[mc % 2][:], pp[:], v["b2"][:, mc:mc + 1], None, ALU.add))
                    self.emit("dve", lambda E, mc=mc: E.tensor_tensor(
                        t15[mc % 2][:], t15[mc % 2][:], xf5[:, mc, :], ALU.add),
                        inc="dve")
                    ev_ev[mc] = self.cnt["dve"]
                    self.wait("sync", "dve", ev_ev[mc])
                    ev_st[stt * KC + mc] = self.dma(
                        "sync", lambda E, stt=stt, mc=mc: E.dma_start(
                            out=s_half(stt, mc),
                            in_=t15[mc % 2][:].rearrange("p (t n) -> p t n", t=2)),
                        "qt%d" % (mc % 2))
                    self._b2st = ev_st
                prev_last = ev_ev[KC - 1]
                if stt + 1 < nt2:
                    ev_x[stt + 1] = self.dma("sync", lambda E, stt=stt: E.dma_start(
                        out=v4(xf5), in_=xa_pair(stt + 1)), "qs")

    def _ffn_up(self, L):
        P = L + "_"
        d = self.d
        T = TILE
        T2 = 2 * T
        nt2 = self.nt // 2
        NH = HID3 // 128
        with contextlib.ExitStack() as st:
            wg = self.sb(P + "F_wg", [128, KC, HID3], BF16, st)
            wu = self.sb(P + "F_wu", [128, KC, HID3], BF16, st)
            load_weight(self, wg, f"{L}_wgT", KC)
            load_weight(self, wu, f"{L}_wuT", KC)
            midb = self.sb(P + "F_mid", [128, NH, T2], BF16, st)
            silu_t = [self.sb(P + f"F_sl{i}", [128, T2], BF16, st) for i in range(3)]
            xnbb = [self.sb(P + f"F_xn{i}", [128, KC, T2], BF16, st) for i in range(2)]
            pj = [self.ps(P + f"F_pj{i}", [128, 512], F32, st) for i in range(6)]
            self.barrier()
            src = d["xn_s"] if L == "L0" else d["xn1_s"]

            def src_pair(stt):
                return src[2 * stt:2 * stt + 2].rearrange("t p c n -> p c t n")

            def mid_pair(stt):
                return d["mid_s"][2 * stt:2 * stt + 2].rearrange(
                    "t p c n -> p c t n")

            def v4(buf):
                return buf[:].rearrange("p c (t n) -> p c t n", t=2)

            ev_ld = {0: self.dma("pool", lambda E: E.dma_start(
                out=v4(xnbb[0]), in_=src_pair(0)), "qp0")}
            for stt in range(nt2):
                xnb = xnbb[stt % 2]
                if stt + 1 < nt2:
                    self.wait("pool", "pe")
                    ev_ld[stt + 1] = self.dma("pool", lambda E, stt=stt: E.dma_start(
                        out=v4(xnbb[(stt + 1) % 2]), in_=src_pair(stt + 1)),
                        "qp%d" % ((stt + 1) % 2))
                self.wait("pe", "qp%d" % (stt % 2), ev_ld[stt])
                self.wait("pe", "dve")
                self.wait("pe", "act")
                self.wait("dve", "qs")
                ev_mid = {}
                for mh in range(NH):
                    pg, pu = pj[2 * (mh % 3)], pj[2 * (mh % 3) + 1]
                    if mh >= 3:
                        self.wait("pe", "dve", ev_mid[mh - 3])
                    for kc in range(KC):
                        self.emit("pe", lambda E, kc=kc, mh=mh, pg=pg, xnb=xnb:
                                  E.matmul(
                            pg[:], wg[:, kc, mh * 128:(mh + 1) * 128], xnb[:, kc, :],
                            start=(kc == 0), stop=(kc == KC - 1)),
                            inc="pe" if kc == KC - 1 else None)
                    ev_g = self.cnt["pe"]
                    for kc in range(KC):
                        self.emit("pe", lambda E, kc=kc, mh=mh, pu=pu, xnb=xnb:
                                  E.matmul(
                            pu[:], wu[:, kc, mh * 128:(mh + 1) * 128], xnb[:, kc, :],
                            start=(kc == 0), stop=(kc == KC - 1)),
                            inc="pe" if kc == KC - 1 else None)
                    ev_u = self.cnt["pe"]
                    self.wait("act", "pe", ev_g)
                    if mh >= 3:
                        self.wait("act", "dve", ev_mid[mh - 3])
                    self.emit("act", lambda E, mh=mh, pg=pg: E.activation(
                        silu_t[mh % 3][:], pg[:], AF.Silu, scale=1.0 / 127.0),
                        inc="act")
                    self.wait("dve", "pe", ev_u)
                    self.wait("dve", "act")
                    self.emit("dve", lambda E, mh=mh, pu=pu: E.scalar_tensor_tensor(
                        midb[:, mh, :], pu[:], 1.0 / 127.0, silu_t[mh % 3][:],
                        ALU.mult, ALU.mult))
                    ev_mid[mh] = self.cnt["dve"]
                self.wait("sync", "dve")
                self.dma("sync", lambda E, stt=stt: E.dma_start(
                    out=mid_pair(stt),
                    in_=midb[:].rearrange("p c (t n) -> p c t n", t=2)), "qs")

    def _ffn_down(self, L, xsrc, ssrc, dst):
        P = L + "_"
        d = self.d
        T = TILE
        T2 = 2 * T
        nt2 = self.nt // 2
        NH = HID3 // 128
        with contextlib.ExitStack() as st:
            wd = self.sb(P + "F_wd", [128, NH, 1024], BF16, st)
            load_weight(self, wd, f"{L}_wdT", NH)
            midbb = [self.sb(P + f"F_mid2{i}", [128, NH, T2], BF16, st)
                     for i in range(2)]
            xf5 = self.sb(P + "F_xf", [128, KC, T2], F32, st)
            sf5 = self.sb(P + "F_sf", [128, KC, T2], F32, st)
            gt = [self.sb(P + f"F_gt{i}", [128, T2], F32, st) for i in range(2)]
            pj = [self.ps(P + f"F2_pj{i}", [128, 512], F32, st) for i in range(2)]
            self.barrier()
            tmp2 = [self.sb(P + f"F_tm{i}", [128, T2], F32, st) for i in range(2)]

            def pair(name, stt):
                return d[name][2 * stt:2 * stt + 2].rearrange("t p c n -> p c t n")

            def v4(buf):
                return buf[:].rearrange("p c (t n) -> p c t n", t=2)

            def dst_half(name, stt, mc):
                return d[name][2 * stt:2 * stt + 2, :, mc, :].rearrange(
                    "t p n -> p t n")

            ev_ld = {0: self.dma("sync", lambda E: E.dma_start(
                out=v4(midbb[0]), in_=pair("mid_s", 0)), "ql0")}
            self.dma("sync", lambda E: E.dma_start(out=v4(xf5), in_=pair(xsrc, 0)),
                     "qs")
            ev_xs = {0: self.dma("sync", lambda E: E.dma_start(
                out=v4(sf5), in_=pair(ssrc, 0)), "qs")}
            for stt in range(nt2):
                midb = midbb[stt % 2]
                if stt + 1 < nt2:
                    self.wait("sync", "pe")
                    ev_ld[stt + 1] = self.dma("sync", lambda E, stt=stt: E.dma_start(
                        out=v4(midbb[(stt + 1) % 2]), in_=pair("mid_s", stt + 1)),
                        "ql%d" % ((stt + 1) % 2))
                self.wait("pe", "ql%d" % (stt % 2), ev_ld[stt])
                self.wait("pe", "act")
                self.wait("act", "dve")
                ev_c = {}
                ev_mul = {}
                for mc in range(KC):
                    pp = pj[mc % 2]
                    if mc >= 2:
                        self.wait("pe", "act", ev_c[mc - 2])
                    for mh in range(NH):
                        self.emit("pe", lambda E, mh=mh, mc=mc, pp=pp, midb=midb:
                                  E.matmul(
                            pp[:], wd[:, mh, mc * 128:(mc + 1) * 128],
                            midb[:, mh, :], start=(mh == 0), stop=(mh == NH - 1)),
                            inc="pe" if mh == NH - 1 else None)
                    ev_mm = self.cnt["pe"]
                    self.wait("act", "pe", ev_mm)
                    if mc >= 2:
                        self.wait("act", "dve", ev_mul[mc - 2])
                    self.emit("act", lambda E, mc=mc, pp=pp: E.activation(
                        gt[mc % 2][:], pp[:], AF.Silu, scale=1.0 / 127.0),
                        inc="act")
                    ev_c[mc] = self.cnt["act"]
                    self.wait("dve", "act", ev_c[mc])
                    if mc == 0:
                        self.wait("dve", "qs", ev_xs[stt])
                    ev_st = getattr(self, "_fdst", {})
                    key = stt * KC + mc
                    if key >= 2:
                        self.wait("dve", "qt%d" % (mc % 2), ev_st[key - 2])
                    self.emit("dve", lambda E, mc=mc: E.tensor_tensor(
                        tmp2[mc % 2][:], sf5[:, mc, :], gt[mc % 2][:], ALU.mult))
                    ev_mul[mc] = self.cnt["dve"]
                    self.emit("dve", lambda E, mc=mc: E.tensor_tensor(
                        tmp2[mc % 2][:], tmp2[mc % 2][:], xf5[:, mc, :], ALU.add),
                        inc="dve")
                    self.wait("sync", "dve")
                    ev_st[key] = self.dma(
                        "sync", lambda E, stt=stt, mc=mc, dst=dst: E.dma_start(
                            out=dst_half(dst, stt, mc),
                            in_=tmp2[mc % 2][:].rearrange("p (t n) -> p t n", t=2)),
                        "qt%d" % (mc % 2))
                    self._fdst = ev_st
                if stt + 1 < nt2:
                    self.dma("sync", lambda E, stt=stt, xsrc=xsrc: E.dma_start(
                        out=v4(xf5), in_=pair(xsrc, stt + 1)), "qs")
                    ev_xs[stt + 1] = self.dma("sync", lambda E, stt=stt, ssrc=ssrc:
                                              E.dma_start(
                        out=v4(sf5), in_=pair(ssrc, stt + 1)), "qs")

    def phaseC(self):
        self._fdst = {}
        _ffn_up(self, "L0")
        self.barrier()
        _ffn_down(self, "L0", "x_in", "s_s", "x1_s")

    def phaseE(self):
        self._fdst = {}
        _ffn_up(self, "L1")
        self.barrier()
        _ffn_down(self, "L1", "x1_s", "s1_s", "out")

    def phaseD(self):
        d, v = self.d, self.vec
        T = TILE
        NE = HID2 // 128
        with contextlib.ExitStack() as st:
            sb = lambda n, s, dt: self.sb(n, s, dt, st)
            ps = lambda n, s, dt=F32: self.ps(n, s, dt, st)
            wh = sb("D_wh", [128, KC, 4096], BF16)
            wqk = sb("D_wqk", [128, KC, QKD], BF16)
            wout = sb("D_wout", [128, NE, 1024], BF16)
            load_weight(self, wh, "L1_attn_Wh", KC)
            load_weight(self, wqk, "L1_attn_Wqk", KC)
            load_weight(self, wout, "L1_attn_Wout", NE)
            x1fb = [sb(f"D_x1f{i}", [128, KC, T], F32) for i in range(2)]
            normb = sb("D_normb", [128, KC, T], BF16)
            hvb = sb("D_hvb", [128, 32, T], BF16)
            qkf = sb("D_qkf", [128, T], F32)
            qvar = [dict(qq=sb(f"D_qq{i}", [128, T], BF16),
                         qkk=sb(f"D_qkk{i}", [128, T], BF16),
                         lq=sb(f"D_lq{i}", [128, T], BF16),
                         lk=sb(f"D_lk{i}", [128, T], BF16)) for i in range(2)]
            v_tm = sb("D_vtm", [128, 2, HID2], BF16)
            attnT = sb("D_attnT", [128, 2, T], BF16)
            lkT = sb("D_lkT", [128, 2, 128], BF16)
            t1 = sb("D_t1", [128, T], F32)
            self.linkv_f = sb("linkv_f", [128, HID2], F32)
            self.linkv_b = sb("linkv_b", [128, HID2], BF16)
            self.emit("pool", lambda E: E.memset(self.linkv_f[:], 0.0), inc="pool")
            self.emit("pool", lambda E: E.memset(self.linkv_b[:], 0.0), inc="pool")
            pstat = ps("D_pstat", [16, 512])
            pj = [ps(f"D_pj{i}", [128, 512]) for i in range(2)]
            psm = ps("D_psm", [128, 512])
            pvt = ps("D_pvt", [128, 2048], BF16)
            pkv = ps("D_pkv", [128, 1024])
            preb = hvb[:, 0:NE, :]
            bh2 = sb("D_bh2", [128, 32], F32)
            bqk2 = sb("D_bqk2", [128, 1], F32)
            lnbb = sb("D_lnbb", [128, KC], BF16)
            biasT_l = sb("D_biasT", [128, 2, 256], BF16)
            self.dma("pool", lambda E, biasT_l=biasT_l: E.dma_start(
                out=biasT_l[:], in_=d["biasT_d"]), "qp")
            self.barrier()
            tmpf = self.tmpf_g
            r1 = self.row1
            ev_load = {0: self.dma("sync", lambda E: E.dma_start(
                out=x1fb[0][:], in_=d["x1_s"][0]), "ql0")}
            self.emit("dve", lambda E: E.tensor_copy(lnbb[:], v["lnb"][:]),
                      inc="dve")
            self.wait("pe", "dve")
            self.wait("pe", "qp")
            for mh in range(32):
                for kc in range(KC):
                    self.emit("pe", lambda E, mh=mh, kc=kc: E.matmul(
                        pj[0][:, mh:mh + 1], wh[:, kc, mh * 128:(mh + 1) * 128],
                        lnbb[:, kc:kc + 1], start=(kc == 0), stop=(kc == KC - 1)),
                        inc="pe" if (mh == 31 and kc == KC - 1) else None)
            for kc in range(KC):
                self.emit("pe", lambda E, kc=kc: E.matmul(
                    pj[1][:, 0:1], wqk[:, kc, :], lnbb[:, kc:kc + 1],
                    start=(kc == 0), stop=(kc == KC - 1)),
                    inc="pe")
            self.wait("dve", "pe")
            self.emit("dve", lambda E: E.tensor_tensor(
                bh2[:], pj[0][:, 0:32], v["bh"][:], ALU.add))
            self.emit("dve", lambda E: E.tensor_tensor(
                bqk2[:], pj[1][:, 0:1], v["bqk"][:], ALU.add), inc="dve")
            self.wait("pe", "dve")
            L = locals()
            EV = {}
            ln_tile(self, pstat, x1fb[0], double=True, qs_sem="ql0",
                    qs_val=ev_load[0])
            for t in range(self.nt):
                _D_s1a(self, t, L, EV)
                if t >= 1:
                    _D_s2(self, t - 1, L, EV)
                _D_s1b(self, t, L, EV)
                if t + 1 < self.nt:
                    h1, h2 = slice(0, KC // 2), slice(KC // 2, KC)
                    self.wait("sync", "dve", EV.get(("s1h", t - 1)))
                    if t >= 1:
                        self.wait("sync", "qx%d" % ((t - 1) % 2),
                                  EV[("xn1q", t - 1)])
                    ev_l1 = self.dma("sync", lambda E, t=t, h1=h1: E.dma_start(
                        out=x1fb[(t + 1) % 2][:, h1, :],
                        in_=d["x1_s"][t + 1][:, h1, :]), "qh%d" % ((t + 1) % 2))
                    self.wait("sync", "dve", EV.get(("s1d", t - 1)))
                    ev_load[t + 1] = self.dma("sync", lambda E, t=t, h2=h2:
                                              E.dma_start(
                        out=x1fb[(t + 1) % 2][:, h2, :],
                        in_=d["x1_s"][t + 1][:, h2, :]), "ql%d" % ((t + 1) % 2))
                    ln_tile(self, pstat, x1fb[(t + 1) % 2], double=True,
                            qs_sem="ql%d" % ((t + 1) % 2), qs_val=ev_load[t + 1],
                            eng="pool", pe_guard=EV.get(("lnpe", t)),
                            half_vals=(ev_l1, ev_load[t + 1]))
                    EV[("lnpe", t + 1)] = self._ln_pe
                    EV[("lnrows", t + 1)] = self._ln_ev[1]
                    EV[("lnAB", t + 1)] = self._ln_evAB[1]
                    EV[("lnABd", t + 1)] = self._ln_evAB[0]
            _D_s2(self, self.nt - 1, L, EV)

    def _D_s1a(self, t, L, EV):
        d, v = self.d, self.vec
        T = TILE
        r1 = self.row1
        x1f = L["x1fb"][t % 2]
        pj, psm = L["pj"], L["psm"]
        tmpf = self.tmpf_g
        lnr = EV.get(("lnrows", t))
        lnab = EV.get(("lnAB", t))
        evA = bcast(self, pj[0], r1(LN["A"]), rbf_pe=EV.get(("evR", t - 1)),
                    ps_act=EV.get(("hv", t - 1)),
                    ps_dve=EV.get(("s1d", t - 2), EV.get(("apl", t - 1))),
                    row_act=lnab)
        evB = bcast(self, pj[1], r1(LN["B"]), rbf_pe=evA,
                    ps_act=EV.get(("hv", t - 1)),
                    ps_dve=EV.get(("s1d", t - 2), EV.get(("apl", t - 1))),
                    row_act=lnab)
        evR = bcast(self, L["pkv"], r1(LN["R2"]), rbf_pe=evB,
                    ps_act=EV.get(("lnrows", t)),
                    ps_dve=EV.get(("s1d", t - 1), EV.get(("apl", t - 1))),
                    row_act=lnr)
        EV[("evR", t)] = evR
        self.wait("dve", "pe", evB)
        self.wait("dve", "qs")
        bA = pj[0][:, 0:T].unsqueeze(1).broadcast_to([128, KC, T])
        bB = pj[1][:, 0:T].unsqueeze(1).broadcast_to([128, KC, T])
        self.emit("dve", lambda E, x1f=x1f, bA=bA: E.tensor_tensor(
            tmpf[:], x1f[:], bA, ALU.mult))
        self.emit("dve", lambda E, x1f=x1f, bB=bB: E.tensor_tensor(
            x1f[:], tmpf[:], bB, ALU.subtract))
        ev_xn1 = self.cnt["dve"]
        self.wait("sync", "dve", ev_xn1)
        self.dma("sync", lambda E, t=t, x1f=x1f: E.dma_start(
            out=d["xn1_s"][t], in_=x1f[:]), "qx%d" % (t % 2))
        EV[("xn1q", t)] = self.cnt["qx%d" % (t % 2)]
        self.wait("dve", "pe", evR)
        if ("qkm", t - 1) in EV:
            self.wait("dve", "pe", EV[("qkm", t - 1)])
        normb = L["normb"]
        for kc in range(KC):
            self.emit("dve", lambda E, kc=kc, x1f=x1f: E.scalar_tensor_tensor(
                normb[:, kc, :], x1f[:, kc, :], v["lnw"][:, kc:kc + 1],
                L["pkv"][:, 0:T], ALU.mult, ALU.mult),
                inc="dve" if kc == KC - 1 else None)
        ev_tm = self.cnt["dve"]
        EV[("apl", t)] = ev_tm
        EV[("nm", t)] = ev_tm

    def _D_s1b(self, t, L, EV):
        d, v = self.d, self.vec
        T = TILE
        normb, hvb, pj, psm = L["normb"], L["hvb"], L["pj"], L["psm"]
        wh, wqk = L["wh"], L["wqk"]
        V = L["qvar"][t % 2]
        ev_nm = EV[("nm", t)]
        if ("s1d", t - 1) in EV:
            self.wait("pe", "dve", EV[("s1d", t - 1)])
        self.wait("pe", "dve", ev_nm)
        for kc in range(KC):
            self.emit("pe", lambda E, kc=kc: E.matmul(
                psm[:, 0:T], wqk[:, kc, :], normb[:, kc, :],
                start=(kc == 0), stop=(kc == KC - 1)),
                inc="pe" if kc == KC - 1 else None)
        qkf = L["qkf"]
        self.wait("act", "pe")
        if ("var", t - 1) in EV:
            self.wait("act", "pool", EV[("var", t - 1)])
        self.emit("act", lambda E, qkf=qkf: E.activation(
            qkf[:], psm[:, 0:T], AF.Silu, bias=L["bqk2"][:, 0:1]), inc="act")
        EV[("qkf", t)] = self.cnt["act"]
        self.wait("pool", "act", EV[("qkf", t)])
        if ("qlm", t - 2) in EV:
            self.wait("pool", "pe", EV[("qlm", t - 2)])
        else:
            self.wait("pool", "pe")
        for name, col in (("qq", 0), ("lq", 1), ("qkk", 2), ("lk", 3)):
            dst = V[name]
            wb = v["qkw"][:, col:col + 1].broadcast_to([128, T])
            bb2 = v["qkb"][:, col:col + 1].broadcast_to([128, T])
            self.emit("pool", lambda E, dst=dst, wb=wb, qkf=qkf: E.tensor_tensor(
                dst[:], qkf[:], wb, ALU.mult))
            self.emit("pool", lambda E, dst=dst, bb2=bb2: E.tensor_tensor(
                dst[:], dst[:], bb2, ALU.add))
        self.emit("pool", lambda E: E.engine_nop(), inc="pool")
        EV[("var", t)] = self.cnt["pool"]
        ev_hv = {}
        self.wait("pe", "dve", EV[("apl", t)])
        if ("hv", t - 1) in EV:
            self.wait("pe", "act", EV[("hv", t - 1)])
        else:
            self.wait("pe", "act")
        for mh in range(32):
            pp = pj[mh % 2]
            if mh >= 2:
                self.wait("pe", "act", ev_hv[mh - 2])
            self.wait("pe", "dve", ev_nm)
            for kc in range(KC):
                self.emit("pe", lambda E, kc=kc, mh=mh, pp=pp: E.matmul(
                    pp[:, 0:T], wh[:, kc, mh * 128:(mh + 1) * 128],
                    normb[:, kc, :], start=(kc == 0), stop=(kc == KC - 1)),
                    inc="pe" if kc == KC - 1 else None)
            ev_mm = self.cnt["pe"]
            self.wait("act", "pe", ev_mm)
            self.emit("act", lambda E, mh=mh, pp=pp: E.activation(
                hvb[:, mh, :], pp[:, 0:T], AF.Silu,
                bias=L["bh2"][:, mh:mh + 1]), inc="act")
            ev_hv[mh] = self.cnt["act"]
        EV[("hv", t)] = ev_hv[31]
        EV[("qkm", t)] = self.cnt["pe"]

    def _D_s2(self, t, L, EV):
        d, v = self.d, self.vec
        T = TILE
        NE = HID2 // 128
        normb, hvb, pj, psm = L["normb"], L["hvb"], L["pj"], L["psm"]
        pvt, pkv = L["pvt"], L["pkv"]
        attnT, lkT, t1, v_tm = L["attnT"], L["lkT"], L["t1"], L["v_tm"]
        preb = L["preb"]
        wout = L["wout"]
        x1f = L["x1fb"][t % 2]
        V = L["qvar"][t % 2]
        qq, qkk, lq, lk = V["qq"], V["qkk"], V["lq"], V["lk"]
        tmpf = self.tmpf_g
        self.wait("pe", "pool", EV[("var", t)])
        if ("s1d", t - 1) in EV:
            self.wait("pe", "dve", EV[("s1d", t - 1)])

        for jt in range(2):
            self.emit("pe", lambda E, jt=jt, qkk=qkk, qq=qq: E.matmul(
                psm[:, jt * T:(jt + 1) * T], qkk[:, jt * 128:(jt + 1) * 128],
                qq[:], start=True, stop=True), inc="pe")
        ev_sim = self.cnt["pe"]
        self.wait("dve", "pe", ev_sim)
        for jt in range(2):
            self.emit("dve", lambda E, jt=jt: E.scalar_tensor_tensor(
                t1[:], psm[:, jt * T:(jt + 1) * T], 1.0 / 256.0,
                L["biasT_l"][:, jt, :], ALU.mult, ALU.add))
            self.wait("act", "dve")
            self.emit("act", lambda E: E.activation(t1[:], t1[:], AF.Relu),
                      inc="act")
            self.emit("act", lambda E, jt=jt: E.activation(
                attnT[:, jt, :], t1[:], AF.Square), inc="act")
            self.wait("dve", "act")
        ev_attn = self.cnt["act"]
        EV[("stt", t)] = self.cnt["dve"]
        for jt in range(2):
            self.wait("pe", "act")
            for ec in range(NE):
                self.emit("pe", lambda E, jt=jt, ec=ec: E.transpose(
                    pvt[:, ec * 128:(ec + 1) * 128],
                    hvb[:, ec, jt * 128:(jt + 1) * 128], self.ident_b[:]),
                    inc="pe" if ec == NE - 1 else None)
            self.wait("act", "pe")
            self.emit("act", lambda E, jt=jt: E.copy(v_tm[:, jt, :], pvt[:]),
                      inc="act")
        self.wait("pe", "act")
        for jt in range(2):
            self.emit("pe", lambda E, jt=jt, lk=lk: E.transpose(
                pvt[:, jt * 128:(jt + 1) * 128], lk[:, jt * 128:(jt + 1) * 128],
                self.ident_b[:]), inc="pe" if jt == 1 else None)
        self.wait("act", "pe")
        self.emit("act", lambda E: E.copy(
            lkT[:], pvt[:, 0:256].rearrange("p (j e) -> p j e", j=2)),
            inc="act")
        ev_lkT = self.cnt["act"]
        ev_pre = {}
        for ec in range(NE):
            pq = pj[ec % 2]
            if ec >= 2:
                self.wait("pe", "dve", ev_pre[ec - 2])
            self.wait("pe", "act", ev_attn)
            for jt in range(2):
                self.emit("pe", lambda E, ec=ec, jt=jt, pq=pq: E.matmul(
                    pq[:, 0:T], v_tm[:, jt, ec * 128:(ec + 1) * 128],
                    attnT[:, jt, :], start=(jt == 0), stop=False))
            self.emit("pe", lambda E, ec=ec, pq=pq, lq=lq: E.matmul(
                pq[:, 0:T], self.linkv_b[:, ec * 128:(ec + 1) * 128], lq[:],
                start=False, stop=True), inc="pe")
            ev_q = self.cnt["pe"]
            self.wait("dve", "pe", ev_q)
            self.emit("dve", lambda E, ec=ec, pq=pq: E.tensor_tensor(
                preb[:, ec, :], pq[:, 0:T], hvb[:, 16 + ec, :], ALU.mult))
            ev_pre[ec] = self.cnt["dve"]
        ev_preb = self.cnt["dve"]
        self.wait("pe", "act", ev_lkT)
        self.wait("pe", "dve", ev_preb)
        EV[("qlm", t)] = self.cnt["pe"]
        for eh in range(2):
            for sl in range(2):
                for jt in range(2):
                    self.emit("pe", lambda E, eh=eh, sl=sl, jt=jt: E.matmul(
                        pkv[:, sl * 512:(sl + 1) * 512],
                        lkT[:, jt, :],
                        v_tm[:, jt, eh * 1024 + sl * 512:eh * 1024 + (sl + 1) * 512],
                        start=(jt == 0), stop=(jt == 1)),
                        inc="pe" if (sl == 1 and jt == 1) else None)
            ev_kv = self.cnt["pe"]
            self.wait("dve", "pe", ev_kv)
            half = slice(eh * 1024, (eh + 1) * 1024)
            self.emit("dve", lambda E, half=half: E.scalar_tensor_tensor(
                self.linkv_f[:, half], pkv[:], 1.0 / 256.0,
                self.linkv_f[:, half], ALU.mult, ALU.add))
            self.emit("dve", lambda E, half=half: E.tensor_copy(
                self.linkv_b[:, half], self.linkv_f[:, half]))
            ev_lkv = self.cnt["dve"]
            EV[("lkv", t)] = ev_lkv
            if eh == 0:
                self.wait("pe", "dve", ev_lkv)
        ev_s1 = {}
        self.wait("pe", "dve")
        self.wait("dve", "qs")
        for mc in range(KC):
            pp = psm if mc % 2 == 0 else L["pkv"]
            if mc >= 2:
                self.wait("pe", "dve", ev_s1[mc - 2])
            for eh2 in range(NE):
                self.emit("pe", lambda E, eh2=eh2, mc=mc, pp=pp: E.matmul(
                    pp[:, 0:T], wout[:, eh2, mc * 128:(mc + 1) * 128],
                    preb[:, eh2, :], start=(eh2 == 0), stop=(eh2 == NE - 1)),
                    inc="pe" if eh2 == NE - 1 else None)
            ev_mm = self.cnt["pe"]
            self.wait("dve", "pe", ev_mm)
            self.emit("dve", lambda E, mc=mc, pp=pp, x1f=x1f:
                      E.scalar_tensor_tensor(
                tmpf[:, mc, :], pp[:, 0:T], v["bout"][:, mc:mc + 1],
                x1f[:, mc, :], ALU.add, ALU.add), inc="dve")
            ev_s1[mc] = self.cnt["dve"]
            if mc == KC // 2 - 1:
                EV[("s1h", t)] = self.cnt["dve"]
        EV[("s1d", t)] = self.cnt["dve"]
        self.wait("sync", "dve")
        self.dma("sync", lambda E, t=t: E.dma_start(out=d["s1_s"][t],
                                                    in_=tmpf[:]), "qs")

    MK.phaseA = phaseA
    MK._A_stage1a = _A_stage1a
    MK._A_stage1b = _A_stage1b
    MK._A_chain = _A_chain
    MK._A_stage2 = _A_stage2
    MK.phaseB = phaseB
    MK.phaseC = phaseC
    MK.phaseD = phaseD
    MK._D_s1a = _D_s1a
    MK._D_s1b = _D_s1b
    MK._D_s2 = _D_s2
    MK.phaseE = phaseE
    MK._ffn_up = _ffn_up
    MK._ffn_down = _ffn_down
    MK._ln_tile = ln_tile
    MK._bcast = bcast
    MK._load_weight = load_weight
    MK._LN = LN


attach_phases(MK)


# version 35
# speedup vs baseline: 1.0595x; 1.0595x over previous
import contextlib
import math
import numpy as np

import concourse.bass as bass
import concourse.mybir as mybir
from concourse.ap import AP
from concourse.bass_utils import run_bass_kernel_spmd

F32 = mybir.dt.float32
BF16 = mybir.dt.bfloat16
I32 = mybir.dt.int32
AF = mybir.ActivationFunctionType
ALU = mybir.AluOpType
AX = mybir.AxisListType

D = 1024
KC = 8
H = 16
DH = 64
CHL = 64
TILE = 256
CPT = TILE // CHL
FFN = 4096
HID3 = 3072
QKD = 128
HID2 = 2048
LN_EPS = 1e-5
ZCLAMP = 44.0
NROW = 21
A_STAGE = 9
A_SUB = 9
A_SUB4 = 9

ENGINES = ("pe", "dve", "act", "pool", "sync")



def _t5_bucket(dd):
    NUM_BUCKETS, MAX_DIST = 32, 128
    max_exact = NUM_BUCKETS // 2
    if dd < max_exact:
        return dd
    ln_ratio = math.log(MAX_DIST / max_exact)
    vv = max_exact + int(math.log(max(dd, 1) / max_exact) / ln_ratio
                         * (NUM_BUCKETS - max_exact))
    return min(vv, NUM_BUCKETS - 1)


def host_statics():
    st = {}
    st["ident"] = np.eye(128, dtype=np.float32)
    m = np.zeros((64, 64), np.float32)
    for tp in range(64):
        m[tp, tp:] = 1.0
    st["triu64"] = m
    hm = np.zeros((16, KC, 128), np.float32)
    for kc in range(KC):
        hm[2 * kc, kc, 0:64] = 1.0
        hm[2 * kc + 1, kc, 64:128] = 1.0
    st["headmap"] = hm
    sq = np.zeros((128, KC, 16), np.float32)
    for kc in range(KC):
        sq[0:64, kc, 2 * kc] = 1.0
        sq[64:128, kc, 2 * kc + 1] = 1.0
    st["sqmap"] = sq
    oh = np.zeros((32, 512), np.float32)
    s = math.sqrt(128.0)
    for k in range(511):
        oh[_t5_bucket(max(k - 255, 0)), k] = s
    st["t5oh"] = oh
    rv = np.zeros((128, 128), np.float32)
    for p in range(128):
        rv[p, 127 - p] = 1.0
    st["rev"] = rv
    mk = np.zeros((128, 2, 256), np.float32)
    for jt in range(2):
        for p in range(128):
            mk[p, jt, :jt * 128 + p] = -1e9
    st["gmask"] = mk
    return st


def host_prep(inputs):
    sh = dict(host_statics())
    w = inputs
    cp = np.ascontiguousarray
    for k in ("L0_attn_Wq", "L0_attn_Wk", "L0_attn_Wv", "L0_attn_Wg", "L0_attn_Wo",
              "L0_attn_W1", "L0_attn_W2", "L1_attn_Wh", "L1_attn_Wqk", "L1_attn_Wout"):
        sh[k] = cp(w[k])
    for L in ("L0", "L1"):
        sh[f"{L}_wgT"] = cp(w[f"{L}_ffn_w_gate"].T)
        sh[f"{L}_wuT"] = cp(w[f"{L}_ffn_w_up"].T)
        sh[f"{L}_wdT"] = cp(w[f"{L}_ffn_w_down"].T)
    sh["L1_attn_rel_table"] = cp(w["L1_attn_rel_table"])
    sh["qkwT"] = cp(w["L1_attn_qk_w"].T)
    sh["qkbT"] = cp(w["L1_attn_qk_b"].T)
    for k in ("L0_attn_bg", "L0_attn_ln1_w", "L0_attn_ln1_b", "L0_attn_ln2_w",
              "L0_attn_ln2_b", "L0_attn_b1", "L0_attn_b2", "L1_attn_ln_w",
              "L1_attn_ln_b", "L1_attn_bh", "L1_attn_bqk", "L1_attn_bout"):
        a = w[k]
        sh[k + "_pc"] = cp(a.reshape(-1, 128).T)

    x = w["L0_attn_x"]
    seq = x.shape[1]
    nt = seq // TILE
    per_core = []
    for b in range(x.shape[0]):
        xt = cp(x[b].T).reshape(KC, 128, nt, TILE).transpose(2, 1, 0, 3)
        per_core.append({"x_in": cp(xt)})
    return sh, per_core



class MK:
    def __init__(self, seq, phases, scratch_out, stack):
        self.nc = bass.Bass("TRN2", target_bir_lowering=False)
        self.seq = seq
        self.nt = seq // TILE
        self.phases = phases
        self.scratch_out = set(scratch_out)
        self.stack = stack
        self.entries = {e: [] for e in ENGINES}
        self.cnt = {}
        self.semh = {}
        self.last_wait = {e: {} for e in ENGINES}
        self._declare_dram()
        self._alloc_global()

    def _sem(self, name):
        self.semh[name] = self.stack.enter_context(self.nc.semaphore(name))
        self.cnt[name] = 0

    AUTO = ("dve", "act", "pool")

    def emit(self, eng, make, inc=None, n=1):
        if eng in self.AUTO:
            self.wait(eng, eng, self.cnt[eng])
            inc = eng
            n = 1
        if inc is not None:
            self.cnt[inc] += n
            self.entries[eng].append(("op", make, self.semh[inc], n))
        else:
            self.entries[eng].append(("op", make, None, 0))
        return self.cnt[inc] if inc else None

    def dma(self, eng, make, q):
        if eng in self.AUTO:
            self.wait(eng, eng, self.cnt[eng])
        self.cnt[q] += 16
        self.entries[eng].append(("op", make, self.semh[q], 16))
        return self.cnt[q]

    def wait(self, eng, sem, val=None):
        if val is None:
            val = self.cnt[sem]
        if val <= 0:
            return
        if val <= self.last_wait[eng].get(sem, 0):
            return
        self.last_wait[eng][sem] = val
        self.entries[eng].append(("wait", self.semh[sem], val, 0))

    def barrier(self):
        snap = dict(self.cnt)
        for e in ENGINES:
            for s, v in snap.items():
                if v > 0:
                    self.wait(e, s, v)

    def sb(self, name, shape, dtype, stk=None):
        return (stk or self.stack).enter_context(self.nc.sbuf_tensor(name, shape, dtype))

    def ps(self, name, shape, dtype=F32, stk=None):
        return (stk or self.stack).enter_context(self.nc.psum_tensor(name, shape, dtype))

    def row1(self, slot):
        return self.rowpool[0:1, slot * TILE:(slot + 1) * TILE]

    def row16(self, slot):
        return self.rowpool[0:16, slot * TILE:(slot + 1) * TILE]

    def _declare_dram(self):
        nc = self.nc
        nt = self.nt
        self.d = {}

        def din(name, shape, dtype=F32):
            self.d[name] = nc.dram_tensor(name, shape, dtype, kind="ExternalInput").ap()

        din("x_in", [nt, 128, KC, TILE])
        din("ident", [128, 128])
        din("triu64", [64, 64])
        din("headmap", [16, KC, 128])
        din("sqmap", [128, KC, 16])
        din("t5oh", [32, 512])
        din("rev", [128, 128])
        din("gmask", [128, 2, 256])
        for k in ("L0_attn_Wq", "L0_attn_Wk", "L0_attn_Wv", "L0_attn_Wg", "L0_attn_Wo"):
            din(k, [1024, 1024])
        din("L0_attn_W1", [1024, 4096])
        din("L0_attn_W2", [4096, 1024])
        for L in ("L0", "L1"):
            din(f"{L}_wgT", [1024, HID3], I32)
            din(f"{L}_wuT", [1024, HID3], I32)
            din(f"{L}_wdT", [HID3, 1024], I32)
        din("L1_attn_Wh", [1024, 4096])
        din("L1_attn_Wqk", [1024, QKD])
        din("L1_attn_Wout", [HID2, 1024])
        for k in ("L0_attn_bg", "L0_attn_ln1_w", "L0_attn_ln1_b", "L0_attn_ln2_w",
                  "L0_attn_ln2_b", "L0_attn_b2", "L1_attn_ln_w", "L1_attn_ln_b",
                  "L1_attn_bout"):
            din(k + "_pc", [128, KC])
        din("L0_attn_b1_pc", [128, 32])
        din("L1_attn_bh_pc", [128, 32])
        din("L1_attn_bqk_pc", [128, 1])
        din("L1_attn_rel_table", [32])
        din("qkwT", [QKD, 4])
        din("qkbT", [QKD, 4])

        def scratch(name, shape):
            kind = "ExternalOutput" if name in self.scratch_out else "Internal"
            self.d[name] = nc.dram_tensor(name, shape, F32, kind=kind).ap()

        self.d["biasT_d"] = nc.dram_tensor(
            "biasT_d", [128, 2, 256], BF16, kind="Internal").ap()
        for name in ("xn_s", "xa_s", "s_s", "x1_s", "xn1_s", "s1_s"):
            scratch(name, [nt, 128, KC, TILE])
        self.d["mid_s"] = nc.dram_tensor(
            "mid_s", [nt, 128, HID3 // 128, TILE], BF16, kind="Internal").ap()
        self.d["u_s"] = nc.dram_tensor(
            "u_s", [nt // 2, 128, 32, 2 * TILE], BF16, kind="Internal").ap()
        self.d["out"] = nc.dram_tensor("out", [nt, 128, KC, TILE], F32,
                                       kind="ExternalOutput").ap()
        self.d["fpad_d"] = nc.dram_tensor("fpad_d", [1, 640], F32, kind="Internal").ap()

    def _alloc_global(self):
        for s in ("pe", "dve", "act", "pool", "qs", "qp",
                  "ql0", "ql1", "qh0", "qh1", "qw0", "qw1", "qx0", "qx1",
                  "qp0", "qp1", "qt0", "qt1"):
            self._sem(s)
        sb = self.sb
        self.ident_b = sb("ident_b", [128, 128], BF16)
        self.triu_b = sb("triu_b", [128, 64], BF16)
        self.hm_b = sb("hm_b", [16, KC, 128], BF16)
        self.sq_b = sb("sq_b", [128, KC, 16], BF16)
        self.ones1_b = sb("ones1_b", [128, 1], BF16)
        self.ones1r_b = sb("ones1r_b", [1, 128], BF16)
        self.rbf = sb("rbf_hilo", [16, 2, TILE], BF16)
        self.zeros64 = sb("zeros64", [128, 64], F32)
        self.epsr = sb("epsr", [1, 1], F32)
        self.rel_f = sb("rel_f", [32, 1], F32)
        self.vec = {}
        for name, ncol in (("bg", KC), ("ln1w", KC), ("ln1b", KC), ("ln2w", KC),
                           ("ln2b", KC), ("b2", KC), ("b1", 32), ("lnw", KC),
                           ("lnb", KC), ("bh", 32), ("bqk", 1), ("bout", KC),
                           ("qkw", 4), ("qkb", 4)):
            self.vec[name] = sb("vec_" + name, [128, ncol], F32)
        self.S_f = sb("S_f", [128, 512], F32)
        self.S_b = sb("S_b", [128, 512], BF16)
        self.xbf_g = sb("xbf_g", [128, KC, TILE], BF16)
        self.sqb_g = sb("sqb_g", [128, KC, TILE], BF16)
        self.tmpf_g = sb("tmpf_g", [128, KC, TILE], F32)
        self.rowpool = sb("rowpool", [16, NROW * TILE], F32)
        self.c44 = sb("c_44", [128, 1], F32)
        self.cm44 = sb("c_m44", [128, 1], F32)
        self.c1e6 = sb("c_1e6", [128, 1], F32)
        self.c1e24 = sb("c_1e24", [128, 1], F32)
        self.chalf = sb("c_half", [128, 1], F32)

    def preamble(self):
        d = self.d
        v = self.vec

        def ld(eng, dst, src, q):
            self.dma(eng, lambda E, dst=dst, src=src: E.dma_start(out=dst, in_=src), q)

        for t, val in ((self.zeros64, 0.0), (self.epsr, LN_EPS), (self.S_f, 0.0),
                       (self.S_b, 0.0),
                       (self.ones1_b, 1.0), (self.ones1r_b, 1.0),
                       (self.c44, ZCLAMP), (self.cm44, -ZCLAMP), (self.c1e6, 1e-6),
                       (self.c1e24, 1e-24), (self.chalf, 0.5)):
            self.emit("pool", lambda E, t=t, val=val: E.memset(t[:], val), inc="pool")
        ld("sync", self.rel_f[:], d["L1_attn_rel_table"].rearrange("(r o) -> r o", o=1), "qs")
        ld("pool", self.ident_b[:], d["ident"], "qp")
        ld("pool", self.triu_b[0:64, :], d["triu64"], "qp")
        ld("pool", self.triu_b[64:128, :], d["triu64"], "qp")
        ld("pool", self.hm_b[:], d["headmap"], "qp")
        ld("pool", self.sq_b[:], d["sqmap"], "qp")
        vecsrc = {"bg": "L0_attn_bg", "ln1w": "L0_attn_ln1_w", "ln1b": "L0_attn_ln1_b",
                  "ln2w": "L0_attn_ln2_w", "ln2b": "L0_attn_ln2_b", "b2": "L0_attn_b2",
                  "b1": "L0_attn_b1", "lnw": "L1_attn_ln_w", "lnb": "L1_attn_ln_b",
                  "bh": "L1_attn_bh", "bout": "L1_attn_bout"}
        for name, src in vecsrc.items():
            ld("sync", v[name][:], d[src + "_pc"], "qs")
        ld("sync", v["bqk"][:], d["L1_attn_bqk_pc"], "qs")
        ld("sync", v["qkw"][:], d["qkwT"], "qs")
        ld("sync", v["qkb"][:], d["qkbT"], "qs")
        self.barrier()

        with contextlib.ExitStack() as st:
            t5oh_f = self.sb("t5oh_f", [32, 512], F32, st)
            rev_b = self.sb("rev_b", [128, 128], BF16, st)
            gmask_f = self.sb("gmask_f", [128, 2, 256], F32, st)
            frow = self.sb("fpad_row", [1, 640], F32, st)
            toep = self.sb("toep", [128, 2, 256], BF16, st)
            pf = self.ps("p_fpad", [1, 512], F32, st)
            pb = self.ps("p_bias", [128, 512], F32, st)
            ld("sync", t5oh_f[:], d["t5oh"], "qs")
            ld("sync", gmask_f[:], d["gmask"], "qs")
            ld("pool", rev_b[:], d["rev"], "qp")
            self.emit("pool", lambda E: E.memset(frow[:], 0.0), inc="pool")
            self.barrier()
            self.emit("pe", lambda E: E.matmul(pf[:], self.rel_f[:], t5oh_f[:],
                                               start=True, stop=True), inc="pe")
            self.wait("dve", "pe")
            self.emit("dve", lambda E: E.tensor_copy(frow[:, 0:512], pf[:]), inc="dve")
            self.wait("sync", "dve")
            self.dma("sync", lambda E: E.dma_start(out=d["fpad_d"], in_=frow[:]), "qs")
            for J in range(2):
                sap = AP(d["fpad_d"].tensor, 128 - J * 128, [[1, 128], [1, 256]])
                self.wait("pool", "qs")
                self.dma("pool", lambda E, J=J, sap=sap: E.dma_start(
                    out=toep[:, J, :], in_=sap), "qp")
            self.wait("pe", "qp")
            for J in range(2):
                self.emit("pe", lambda E, J=J: E.matmul(
                    pb[:, J * 256:(J + 1) * 256], rev_b[:], toep[:, J, :],
                    start=True, stop=True), inc="pe")
            biasT_t = self.sb("biasT_t", [128, 2, 256], BF16, st)
            self.wait("dve", "pe")
            self.emit("dve", lambda E, biasT_t=biasT_t: E.tensor_tensor(
                biasT_t[:], pb[:].rearrange("p (j i) -> p j i", j=2),
                gmask_f[:], ALU.add), inc="dve")
            self.wait("sync", "dve")
            self.dma("sync", lambda E, biasT_t=biasT_t: E.dma_start(
                out=d["biasT_d"], in_=biasT_t[:]), "qs")
            self.barrier()

    def finish(self):
        nc = self.nc
        with nc.Block() as block:
            def runner(entries):
                def run(E):
                    for kind, a, b, c in entries:
                        if kind == "op":
                            ins = a(E)
                            if b is not None:
                                ins.then_inc(b, c)
                        else:
                            E.wait_ge(a, b)
                return run
            block.tensor(runner(self.entries["pe"]))
            block.vector(runner(self.entries["dve"]))
            block.scalar(runner(self.entries["act"]))
            block.gpsimd(runner(self.entries["pool"]))
            block.sync(runner(self.entries["sync"]))
        return nc


def build(seq, phases=("A", "B", "C", "D", "E"), scratch_out=()):
    with contextlib.ExitStack() as stack:
        mk = MK(seq, phases, scratch_out, stack)
        mk.preamble()
        for ph in phases:
            getattr(mk, "phase" + ph)()
            mk.barrier()
        nc = mk.finish()
    return nc


def kernel(**inputs):
    sh, per_core = host_prep(inputs)
    seq = inputs["L0_attn_x"].shape[1]
    nc = build(seq)
    in_maps = []
    for pc in per_core:
        m = dict(sh)
        m.update(pc)
        in_maps.append(m)
    res = run_bass_kernel_spmd(nc, in_maps, core_ids=list(range(len(per_core))))
    outs = []
    for r in res.results:
        o = np.asarray(r["out"])
        o = o.transpose(2, 1, 0, 3).reshape(D, seq)
        outs.append(o.T)
    return np.stack(outs, 0).astype(np.float32)


def attach_phases(MK):
    LN = dict(mean=0, msq=1, m2=2, var=3, lv=4, A=5, B=6, ve=7, rec=8, u=9,
              lu=10, R2=11)

    def load_weight(self, dst, src_name, nk):
        src = self.d[src_name]
        for kc in range(nk):
            self.dma("pool", lambda E, kc=kc, dst=dst, src=src: E.dma_start(
                out=dst[:, kc, :], in_=src[kc * 128:(kc + 1) * 128, :]), "qp")

    def ln_tile(self, psS, xf, double=False, qs_sem="qs", qs_val=None,
                eng="dve", pe_guard=None, half_vals=None):
        T = TILE
        r1 = self.row1
        KH = KC // 2

        def sq_emit(dst, src):
            if eng == "pool":
                self.emit("pool", lambda E, dst=dst, src=src: E.tensor_tensor(
                    dst, src, src, ALU.mult), inc="pool")
                return ("pool", self.cnt["pool"])
            self.wait("act", eng)
            self.emit("act", lambda E, dst=dst, src=src: E.activation(
                dst, src, AF.Square), inc="act")
            return ("act", self.cnt["act"])

        if half_vals is None:
            halves = [(qs_sem, qs_val, slice(0, KC))]
        else:
            halves = [("qh" + qs_sem[-1], half_vals[0], slice(0, KH)),
                      (qs_sem, half_vals[1], slice(KH, KC))]
        ev_cp = []
        ev_sq = []
        for i, (sem, val, h) in enumerate(halves):
            self.wait(eng, sem, val)
            if i == 0:
                if qs_sem != "qs":
                    self.wait(eng, "qs")
                self.wait(eng, "pe", pe_guard)
            self.emit(eng, lambda E, h=h: E.tensor_copy(
                self.xbf_g[:, h, :], xf[:, h, :]), inc=eng)
            ev_cp.append((eng, self.cnt[eng]))
            ev_sq.append(sq_emit(self.sqb_g[:, h, :], self.xbf_g[:, h, :]))
        self.wait("pe", *ev_cp[0])
        for kc in range(KC):
            if half_vals is not None and kc == KH:
                self.wait("pe", *ev_cp[1])
            self.emit("pe", lambda E, kc=kc: E.matmul(
                psS[0:1, 0:T], self.ones1_b[:, 0:1], self.xbf_g[:, kc, :],
                start=(kc == 0), stop=(kc == KC - 1)))
        self.wait("pe", *ev_sq[-1])
        for kc in range(KC):
            self.emit("pe", lambda E, kc=kc: E.matmul(
                psS[0:1, T:2 * T], self.ones1_b[:, 0:1], self.sqb_g[:, kc, :],
                start=(kc == 0), stop=(kc == KC - 1)),
                inc="pe" if kc == KC - 1 else None)
        ev_stats = self.cnt["pe"]
        self._ln_pe = ev_stats
        self.wait("act", "pe", ev_stats)
        self.emit("act", lambda E: E.activation(r1(LN["mean"]), psS[0:1, 0:T],
                                                AF.Copy, scale=1.0 / D))
        self.emit("act", lambda E: E.activation(r1(LN["msq"]), psS[0:1, T:2 * T],
                                                AF.Copy, scale=1.0 / D), inc="act")
        self._ln_pstat_ev = self.cnt["act"]
        self.wait("dve", "act")
        self.emit("dve", lambda E: E.tensor_tensor(r1(LN["m2"]), r1(LN["mean"]),
                                                   r1(LN["mean"]), ALU.mult))
        self.emit("dve", lambda E: E.tensor_tensor(r1(LN["var"]), r1(LN["msq"]),
                                                   r1(LN["m2"]), ALU.subtract), inc="dve")
        self.wait("act", "dve")
        self.emit("act", lambda E: E.activation(r1(LN["lv"]), r1(LN["var"]), AF.Ln,
                                                bias=self.epsr[0:1, 0:1]))
        self.emit("act", lambda E: E.activation(r1(LN["A"]), r1(LN["lv"]), AF.Exp,
                                                scale=-0.5), inc="act")
        self.wait("dve", "act")
        self.emit("dve", lambda E: E.tensor_tensor(r1(LN["B"]), r1(LN["mean"]),
                                                   r1(LN["A"]), ALU.mult), inc="dve")
        self._ln_evAB = (self.cnt["dve"], self.cnt["act"])
        if double:
            self.emit("dve", lambda E: E.tensor_scalar(
                r1(LN["ve"]), r1(LN["var"]), self.epsr[0:1, 0:1], None, ALU.add))
            self.emit("dve", lambda E: E.reciprocal(r1(LN["rec"]), r1(LN["ve"])))
            self.emit("dve", lambda E: E.tensor_tensor(r1(LN["u"]), r1(LN["var"]),
                                                       r1(LN["rec"]), ALU.mult), inc="dve")
            self.wait("act", "dve")
            self.emit("act", lambda E: E.activation(r1(LN["lu"]), r1(LN["u"]), AF.Ln,
                                                    bias=self.epsr[0:1, 0:1]))
            self.emit("act", lambda E: E.activation(r1(LN["R2"]), r1(LN["lu"]), AF.Exp,
                                                    scale=-0.5), inc="act")
        self._ln_ev = (self.cnt["dve"], self.cnt["act"])

    def bcast(self, psum, row, rbf_pe=None, ps_act=None, ps_dve=None,
              row_act=None):
        hi = self.rbf[0:1, 0, :]
        lo = self.rbf[0:1, 1, :]
        self.wait("dve", "pe", rbf_pe)
        if row_act is not None:
            self.wait("dve", "act", row_act)
        self.emit("dve", lambda E, row=row, hi=hi: E.tensor_copy(hi, row))
        self.emit("dve", lambda E, row=row, hi=hi, lo=lo: E.tensor_tensor(
            lo, row, hi, ALU.subtract), inc="dve")
        ev_hl = self.cnt["dve"]
        self.wait("pe", "dve", ev_hl if ps_dve is None else max(ev_hl, ps_dve))
        self.wait("pe", "act", ps_act)
        self.emit("pe", lambda E, psum=psum, hi=hi: E.matmul(
            psum[:, 0:TILE], self.ones1r_b[0:1, :], hi, start=True, stop=False))
        self.emit("pe", lambda E, psum=psum, lo=lo: E.matmul(
            psum[:, 0:TILE], self.ones1r_b[0:1, :], lo, start=False, stop=True),
            inc="pe")
        return self.cnt["pe"]

    def phaseA(self):
        d, v = self.d, self.vec
        T = TILE
        with contextlib.ExitStack() as st:
            sb = lambda n, s, dt: self.sb(n, s, dt, st)
            ps = lambda n, s, dt=F32: self.ps(n, s, dt, st)
            wq = sb("A_wq", [128, KC, 1024], BF16)
            wk = sb("A_wk", [128, KC, 1024], BF16)
            wv = sb("A_wv", [128, KC, 1024], BF16)
            wg = sb("A_wg", [128, KC, 1024], BF16)
            wo = sb("A_wo", [128, KC, 1024], BF16)
            for name, dst in (("L0_attn_Wq", wq), ("L0_attn_Wk", wk),
                              ("L0_attn_Wv", wv), ("L0_attn_Wg", wg),
                              ("L0_attn_Wo", wo)):
                load_weight(self, dst, name, KC)
            xfb = [sb(f"A_xf{i}", [128, KC, T], F32) for i in range(2)]
            xn2b = sb("A_xn2b", [128, KC, T], BF16)
            qbb = [sb(f"A_qb{i}", [128, KC, T], BF16) for i in range(2)]
            kbb = [sb(f"A_kb{i}", [128, KC, T], BF16) for i in range(2)]
            qsim = sb("A_qsim", [128, KC, T], BF16)
            ksim = sb("A_ksim", [128, KC, T], BF16)
            lacum = sb("A_lacum", [128, KC, T], F32)
            sqmc = [sb(f"A_sqmc{i}", [128, T], BF16) for i in range(2)]
            alf = sb("A_alf", [128, KC, T], F32)
            cdb = [sb(f"A_cd{i}", [128, KC, CPT], F32) for i in range(2)]
            kdtm = sb("A_kdtm", [128, 1024], BF16)
            v_tm = sb("A_vtm", [128, 2, 1024], BF16)
            oT = sb("A_oT", [128, KC, T], BF16)
            mid = sb("A_mid", [16, CPT], F32)
            midt = sb("A_midt", [16, CPT], F32)
            rqf = sb("A_rqf", [16, T], BF16)
            rkf = sb("A_rkf", [16, T], BF16)
            rkb = sb("A_rkb", [128, KC, T], BF16)
            kvsb = sb("A_kvsb", [128, 512], F32)
            pstat = ps("A_pstat", [16, 512])
            pj = [ps(f"A_pj{i}", [128, 512]) for i in range(2)]
            psq = ps("A_psq", [16, 512])
            psim = ps("A_psim", [128, 512])
            pso = ps("A_pso", [128, 512])
            pskv = ps("A_pskv", [128, 512])
            pskt = ps("A_pskt", [64, 1024], BF16)
            bq2 = sb("A_bq2", [128, KC], F32)
            bk2 = sb("A_bk2", [128, KC], F32)
            bg2 = sb("A_bg2", [128, KC], F32)
            bvr = sb("A_bvr", [1, 1024], BF16)
            lnbb = sb("A_lnbb", [128, KC], BF16)
            self.barrier()
            ev_load = {0: self.dma("sync", lambda E: E.dma_start(
                out=xfb[0][:], in_=d["x_in"][0]), "ql0")}
            self.emit("dve", lambda E: E.tensor_copy(lnbb[:], v["ln1b"][:]),
                      inc="dve")
            self.wait("pe", "dve")
            self.wait("pe", "qp")
            for wi, w8 in enumerate((wq, wk, wg)):
                pp = pj[wi % 2]
                self.wait("pe", "dve")
                for mc in range(KC):
                    for kc in range(KC):
                        self.emit("pe", lambda E, w8=w8, mc=mc, kc=kc, pp=pp:
                                  E.matmul(
                            pp[:, mc:mc + 1], w8[:, kc, mc * 128:(mc + 1) * 128],
                            lnbb[:, kc:kc + 1], start=(kc == 0),
                            stop=(kc == KC - 1)),
                            inc="pe" if (mc == KC - 1 and kc == KC - 1) else None)
                evb0 = self.cnt["pe"]
                self.wait("dve", "pe", evb0)
                dst = (bq2, bk2, bg2)[wi]
                if wi == 2:
                    self.emit("dve", lambda E, dst=dst, pp=pp: E.tensor_tensor(
                        dst[:], pp[:, 0:KC], v["bg"][:], ALU.add), inc="dve")
                else:
                    self.emit("dve", lambda E, dst=dst, pp=pp: E.tensor_copy(
                        dst[:], pp[:, 0:KC]), inc="dve")
            for ns in range(2):
                self.wait("pe", "dve")
                for kc in range(KC):
                    self.emit("pe", lambda E, ns=ns, kc=kc: E.matmul(
                        pso[0:1, 0:512], lnbb[:, kc:kc + 1],
                        wv[:, kc, ns * 512:(ns + 1) * 512],
                        start=(kc == 0), stop=(kc == KC - 1)),
                        inc="pe")
                self.wait("dve", "pe")
                self.emit("dve", lambda E, ns=ns: E.tensor_copy(
                    bvr[:, ns * 512:(ns + 1) * 512], pso[0:1, 0:512]), inc="dve")
            self.wait("pe", "dve")
            L = locals()
            EV = {}
            ln_tile(self, L["pstat"], xfb[0], double=True, qs_sem="ql0",
                    qs_val=ev_load[0])
            for t in range(self.nt):
                _A_stage1a(self, t, L, ev_load, EV)
                _A_stage1b(self, t, L, ev_load, EV)
                if t >= 1:
                    _A_stage2(self, t - 1, L, EV)
                if t + 1 < self.nt:
                    h1, h2 = slice(0, KC // 2), slice(KC // 2, KC)
                    if t >= 1:
                        self.wait("sync", "qw%d" % ((t - 1) % 2),
                                  EV[("woq", t - 1, 0)])
                    ev_l1 = self.dma("sync", lambda E, t=t, h1=h1: E.dma_start(
                        out=xfb[(t + 1) % 2][:, h1, :],
                        in_=d["x_in"][t + 1][:, h1, :]), "qh%d" % ((t + 1) % 2))
                    self.wait("sync", "qs", EV.get(("woq", t - 1, KC // 2)))
                    ev_load[t + 1] = self.dma("sync", lambda E, t=t, h2=h2:
                                              E.dma_start(
                        out=xfb[(t + 1) % 2][:, h2, :],
                        in_=d["x_in"][t + 1][:, h2, :]), "ql%d" % ((t + 1) % 2))
                if t + 1 < self.nt:
                    ln_tile(self, pstat, xfb[(t + 1) % 2], double=True,
                            qs_sem="ql%d" % ((t + 1) % 2), qs_val=ev_load[t + 1],
                            half_vals=(ev_l1, ev_load[t + 1]))
                    EV[("lnact", t + 1)] = self._ln_pstat_ev
                    EV[("lnAB", t + 1)] = self._ln_evAB[1]
                    EV[("lnrows", t + 1)] = self._ln_ev[1]
                _A_chain(self, t, L, EV)
            _A_stage2(self, self.nt - 1, L, EV)

    def _A_stage1a(self, t, L, ev_load, EV):
        d, v = self.d, self.vec
        T = TILE
        r1 = self.row1
        xn2b, qb, kb, lacum, sqmc, alf = (L["xn2b"], L["qbb"][t % 2],
                                          L["kbb"][t % 2], L["lacum"],
                                          L["sqmc"], L["alf"])
        xf = L["xfb"][t % 2]
        pstat, pj, psq, pso = L["pstat"], L["pj"], L["psq"], L["pso"]
        wq, wk, wg = L["wq"], L["wk"], L["wg"]
        tmpf = self.tmpf_g

        zmm = EV.get(("zmm", t - 1))
        cpz = EV.get(("cpz", t - 1))
        apl = EV.get(("wo", t - 2), EV.get(("apl", t - 1)))
        lnab = EV.get(("lnAB", t))
        lnr = EV.get(("lnrows", t))
        evA = bcast(self, pj[0], r1(LN["A"]), rbf_pe=zmm, ps_act=cpz, ps_dve=apl,
                    row_act=lnab)
        evB = bcast(self, pj[1], r1(LN["B"]), rbf_pe=evA, ps_act=cpz, ps_dve=apl,
                    row_act=lnab)
        evR = bcast(self, pso, r1(LN["R2"]), rbf_pe=evB, ps_act=cpz, ps_dve=apl,
                    row_act=lnr)
        self.wait("dve", "pe", evB)
        bA = pj[0][:, 0:T].unsqueeze(1).broadcast_to([128, KC, T])
        bB = pj[1][:, 0:T].unsqueeze(1).broadcast_to([128, KC, T])
        bR = pso[:, 0:T].unsqueeze(1).broadcast_to([128, KC, T])
        if ("zfree", t - 1) in EV:
            self.wait("dve", "pool", EV[("zfree", t - 1)])
        else:
            self.wait("dve", "pool")
        self.emit("dve", lambda E, xf=xf, bA=bA: E.tensor_tensor(
            tmpf[:], xf[:], bA, ALU.mult))
        self.emit("dve", lambda E, bB=bB, xf=xf: E.tensor_tensor(
            xf[:], tmpf[:], bB, ALU.subtract), inc="dve")
        ev_xn = self.cnt["dve"]
        self.wait("sync", "dve", ev_xn)
        self.dma("sync", lambda E, xf=xf, t=t: E.dma_start(
            out=d["xn_s"][t], in_=xf[:]), "qs")
        self.wait("dve", "pe", evR)
        if ("vmm", t - 1) in EV:
            self.wait("dve", "pe", EV[("vmm", t - 1)])
        for kc in range(KC):
            self.emit("dve", lambda E, kc=kc, xf=xf: E.scalar_tensor_tensor(
                xn2b[:, kc, :], xf[:, kc, :], v["ln1w"][:, kc:kc + 1],
                pso[:, 0:T], ALU.mult, ALU.mult),
                inc="dve" if kc == KC - 1 else None)
        ev_xn2 = self.cnt["dve"]
        EV[("apl", t)] = ev_xn2
        EV[("xn2", t)] = ev_xn2

    def _A_stage1b(self, t, L, ev_load, EV):
        d, v = self.d, self.vec
        T = TILE
        xn2b, qb, kb, lacum, sqmc, alf = (L["xn2b"], L["qbb"][t % 2],
                                          L["kbb"][t % 2], L["lacum"],
                                          L["sqmc"], L["alf"])
        pj, psq = L["pj"], L["psq"]
        wq, wk, wg = L["wq"], L["wk"], L["wg"]
        ev_xn2 = EV[("xn2", t)]
        if ("wo", t - 1) in EV:
            self.wait("pe", "dve", EV[("wo", t - 1)])

        def proj_qk(w, dstb, qhalf):
            ev_evac = {}
            off = 0 if qhalf else T

            def ssq_mm(mc):
                self.wait("pe", "act", ev_evac[mc])
                self.emit("pe", lambda E, mc=mc: E.matmul(
                    psq[0:16, off:off + T], self.sq_b[:, mc, :], sqmc[mc % 2][:],
                    start=(mc == 0), stop=(mc == KC - 1)),
                    inc="pe" if mc == KC - 1 else None)

            for mc in range(KC):
                pp = pj[mc % 2]
                if mc >= 2:
                    self.wait("pe", "act", ev_evac[mc - 2])
                self.wait("pe", "dve", ev_xn2)
                for kc in range(KC):
                    self.emit("pe", lambda E, kc=kc, mc=mc, pp=pp, w=w: E.matmul(
                        pp[:, 0:T], w[:, kc, mc * 128:(mc + 1) * 128], xn2b[:, kc, :],
                        start=(kc == 0), stop=(kc == KC - 1)),
                        inc="pe" if kc == KC - 1 else None)
                evmm = self.cnt["pe"]
                self.wait("act", "pe", evmm)
                bcol = L["bq2"] if qhalf else L["bk2"]
                self.emit("act", lambda E, mc=mc, pp=pp, dstb=dstb, bcol=bcol:
                          E.activation(
                    dstb[:, mc, :], pp[:, 0:T], AF.Identity,
                    bias=bcol[:, mc:mc + 1]))
                self.emit("act", lambda E, mc=mc, pp=pp, bcol=bcol: E.activation(
                    sqmc[mc % 2][:], pp[:, 0:T], AF.Square,
                    bias=bcol[:, mc:mc + 1]), inc="act")
                ev_evac[mc] = self.cnt["act"]
                if mc >= 1:
                    ssq_mm(mc - 1)
            ssq_mm(KC - 1)
            return self.cnt["pe"]

        if ("rf", t - 1) in EV:
            self.wait("pe", "dve", EV[("rf", t - 1)])
        EV[("ssq_q", t)] = proj_qk(wq, qb, True)
        EV[("ssq_k", t)] = proj_qk(wk, kb, False)

        ev_sig = {}
        if ("alfp", t - 1) in EV:
            self.wait("act", "pool", EV[("alfp", t - 1)])
        else:
            self.wait("act", "pool")
        for mc in range(KC):
            pp = pj[mc % 2]
            if mc >= 2:
                self.wait("pe", "act", ev_sig[mc - 2])
            self.wait("pe", "dve", ev_xn2)
            for kc in range(KC):
                self.emit("pe", lambda E, kc=kc, mc=mc, pp=pp: E.matmul(
                    pp[:, 0:T], wg[:, kc, mc * 128:(mc + 1) * 128], xn2b[:, kc, :],
                    start=(kc == 0), stop=(kc == KC - 1)),
                    inc="pe" if kc == KC - 1 else None)
            evmm = self.cnt["pe"]
            self.wait("act", "pe", evmm)
            self.emit("act", lambda E, mc=mc, pp=pp: E.activation(
                alf[:, mc, :], pp[:, 0:T], AF.Sigmoid,
                bias=L["bg2"][:, mc:mc + 1]), inc="act")
            ev_sig[mc] = self.cnt["act"]
        EV[("sig", t)] = ev_sig[KC - 1]
        self.wait("dve", "act", ev_sig[KC - 1])
        self.emit("dve", lambda E: E.tensor_scalar(
            alf[:], alf[:], self.c1e6[:, 0:1], None, ALU.max), inc="dve")
        self.wait("act", "dve")
        self.emit("act", lambda E: E.activation(alf[:], alf[:], AF.Ln), inc="act")
        self.wait("dve", "act")
        for mc in range(KC):
            for ch in range(CPT):
                self.emit("dve", lambda E, mc=mc, ch=ch: E.tensor_tensor_scan(
                    lacum[:, mc, ch * CHL:(ch + 1) * CHL],
                    alf[:, mc, ch * CHL:(ch + 1) * CHL], self.zeros64[:],
                    0.0, ALU.add, ALU.add),
                    inc="dve" if (mc == KC - 1 and ch == CPT - 1) else None)
        EV[("lacum", t)] = self.cnt["dve"]

    def _A_chain(self, t, L, EV):
        d, v = self.d, self.vec
        T = TILE
        r16 = self.row16
        (qb, kb, qsim, ksim, lacum, alf, cd, v_tm, mid, midt, rqf, rkf, rkb) = (
            L["qbb"][t % 2], L["kbb"][t % 2], L["qsim"], L["ksim"], L["lacum"],
            L["alf"], L["cdb"][t % 2], L["v_tm"], L["mid"], L["midt"],
            L["rqf"], L["rkf"], L["rkb"])
        pstat, pj, psq = L["pstat"], L["pj"], L["psq"]
        psim, pskv = L["psim"], L["pskv"]
        wv = L["wv"]
        xn2b = L["xn2b"]
        tmpf = self.tmpf_g
        (S_lam, S_z2, S_z, S_eqf, S_ekf, S_ssqm, S_lnr, S_rnq,
         S_rnk) = range(12, 21)
        ev_lacum = EV[("lacum", t)]
        ev_xn2 = EV[("xn2", t)]

        ev_vt = {}
        for i, (m, ns) in enumerate(((0, 0), (0, 1), (1, 0), (1, 1))):
            pp = pj[i % 2]
            if i == 0:
                self.wait("pe", "act", EV[("sig", t)])
                self.wait("pe", "dve", ev_xn2)
                if ("wo", t - 1) in EV:
                    self.wait("pe", "dve", EV[("wo", t - 1)])
            if i >= 2:
                self.wait("pe", "act", ev_vt[i - 2])
            for kc in range(KC):
                self.emit("pe", lambda E, kc=kc, m=m, ns=ns, pp=pp: E.matmul(
                    pp[:, 0:512], xn2b[:, kc, m * 128:(m + 1) * 128],
                    wv[:, kc, ns * 512:(ns + 1) * 512],
                    start=(kc == 0), stop=False))
            self.emit("pe", lambda E, ns=ns, pp=pp: E.matmul(
                pp[:, 0:512], self.ones1r_b[0:1, :],
                L["bvr"][:, ns * 512:(ns + 1) * 512],
                start=False, stop=True), inc="pe")
            self.wait("act", "pe")
            self.emit("act", lambda E, m=m, ns=ns, pp=pp: E.activation(
                v_tm[:, m, ns * 512:(ns + 1) * 512], pp[:, 0:512], AF.Copy),
                inc="act")
            ev_vt[i] = self.cnt["act"]
        ev_vtm = self.cnt["act"]
        EV[("vtm", t)] = ev_vtm
        EV[("vmm", t)] = self.cnt["pe"]

        self.wait("act", "dve", ev_lacum)
        self.emit("act", lambda E, cd=cd: E.activation(
            cd[:], lacum[:, :, CHL - 1::CHL], AF.Exp), inc="act")
        EV[("cd", t)] = self.cnt["act"]

        lacb = L["rkb"]
        self.emit("dve", lambda E, lacb=lacb: E.tensor_copy(lacb[:], lacum[:]),
                  inc="dve")
        self.wait("pe", "dve")
        if ("lnact", t + 1) in EV:
            self.wait("pe", "act", EV[("lnact", t + 1)])
        for kc in range(KC):
            self.emit("pe", lambda E, kc=kc, lacb=lacb: E.matmul(
                pstat[0:16, 0:T], self.sq_b[:, kc, :], lacb[:, kc, :],
                start=(kc == 0), stop=(kc == KC - 1)),
                inc="pe" if kc == KC - 1 else None)
        self.wait("act", "pe")
        self.emit("act", lambda E: E.activation(r16(S_lam), pstat[0:16, 0:T], AF.Copy,
                                                scale=1.0 / DH), inc="act")
        self.wait("dve", "act")
        lam = r16(S_lam)
        self.emit("dve", lambda E: E.tensor_tensor(midt[:], lam[:, 0::CHL],
                                                   lam[:, CHL - 1::CHL], ALU.add))
        self.emit("dve", lambda E: E.tensor_scalar(mid[:], midt[:],
                                                   self.chalf[0:16, 0:1], None, ALU.mult))
        self.emit("dve", lambda E: E.tensor_tensor(
            r16(S_z2).rearrange("p (c k) -> p c k", k=CHL),
            lam.rearrange("p (c k) -> p c k", k=CHL),
            mid[:].unsqueeze(2).broadcast_to([16, CPT, CHL]), ALU.subtract))
        self.emit("dve", lambda E: E.tensor_scalar(
            r16(S_z), r16(S_z2), self.c44[0:16, 0:1], self.cm44[0:16, 0:1],
            ALU.min, ALU.max), inc="dve")
        self.wait("act", "dve")
        self.emit("act", lambda E: E.activation(r16(S_eqf), r16(S_z), AF.Exp))
        self.emit("act", lambda E: E.activation(r16(S_ekf), r16(S_z), AF.Exp,
                                                scale=-1.0), inc="act")
        r2 = lambda sl: self.rowpool[0:16, sl * T:(sl + 2) * T]
        self.wait("dve", "pe", EV[("ssq_k", t)])
        self.wait("dve", "act")
        self.emit("dve", lambda E, r2=r2: E.tensor_scalar(
            r2(S_ssqm), psq[0:16, 0:2 * T], self.c1e24[0:16, 0:1], None,
            ALU.max), inc="dve")
        self.wait("act", "dve")
        self.emit("act", lambda E, r2=r2: E.activation(r2(S_ssqm), r2(S_ssqm),
                                                       AF.Ln))
        self.emit("act", lambda E, r2=r2: E.activation(
            r2(S_rnq), r2(S_ssqm), AF.Exp, scale=-0.5), inc="act")
        self.wait("dve", "act")
        self.emit("dve", lambda E: E.tensor_tensor(rqf[:], r16(S_eqf), r16(S_rnq),
                                                   ALU.mult))
        self.emit("dve", lambda E: E.tensor_tensor(rkf[:], r16(S_ekf), r16(S_rnk),
                                                   ALU.mult), inc="dve")
        EV[("rf", t)] = self.cnt["dve"]

        rqb = self.xbf_g
        zb = tmpf
        zhi = self.rbf[0:16, 0, :]
        zlo = self.rbf[0:16, 1, :]
        self.wait("dve", "pe")
        self.emit("dve", lambda E, zhi=zhi: E.tensor_copy(zhi, r16(S_z)))
        self.emit("dve", lambda E, zhi=zhi, zlo=zlo: E.tensor_tensor(
            zlo, r16(S_z), zhi, ALU.subtract), inc="dve")
        ev_zsplit = self.cnt["dve"]
        ev_zc = {}
        for cyc in range(KC // 2):
            pz = psim if cyc % 2 == 0 else pskv
            if cyc >= 2:
                self.wait("pe", "dve", ev_zc[cyc - 2])
            elif cyc == 0:
                self.wait("pe", "dve", ev_zsplit)
                if ("attnc", t - 1) in EV:
                    self.wait("pe", "act", EV[("attnc", t - 1)])
            for sub2 in range(2):
                mc = 2 * cyc + sub2
                off = sub2 * T
                self.emit("pe", lambda E, mc=mc, pz=pz, off=off, zhi=zhi: E.matmul(
                    pz[:, off:off + T], self.hm_b[:, mc, :], zhi, start=True,
                    stop=False))
                self.emit("pe", lambda E, mc=mc, pz=pz, off=off, zlo=zlo: E.matmul(
                    pz[:, off:off + T], self.hm_b[:, mc, :], zlo, start=False,
                    stop=True), inc="pe" if sub2 == 1 else None)
            evz = self.cnt["pe"]
            self.wait("dve", "pe", evz)
            self.emit("dve", lambda E, cyc=cyc, pz=pz: E.tensor_copy(
                zb[:, 2 * cyc:2 * cyc + 2, :],
                pz[:, 0:2 * T].rearrange("p (a n) -> p a n", a=2)), inc="dve")
            ev_zc[cyc] = self.cnt["dve"]
        ev_zbd = self.cnt["dve"]
        EV[("zbd", t)] = ev_zbd
        ev_cp = {}
        for mc in range(KC):
            pqk = pj[mc % 2]
            if mc >= 2:
                self.wait("pe", "act", ev_cp[mc - 2])
            else:
                self.wait("pe", "act", ev_vtm)
                if ("wo", t - 1) in EV:
                    self.wait("pe", "dve", EV[("wo", t - 1)])
            self.emit("pe", lambda E, mc=mc, pqk=pqk: E.matmul(
                pqk[:, 0:T], self.hm_b[:, mc, :], rqf[:], start=True, stop=True))
            self.emit("pe", lambda E, mc=mc, pqk=pqk: E.matmul(
                pqk[:, T:2 * T], self.hm_b[:, mc, :], rkf[:], start=True, stop=True),
                inc="pe")
            evb = self.cnt["pe"]
            self.wait("act", "pe", evb)
            self.emit("act", lambda E, mc=mc, pqk=pqk: E.activation(
                rqb[:, mc, :], pqk[:, 0:T], AF.Copy))
            self.emit("act", lambda E, mc=mc, pqk=pqk: E.activation(
                rkb[:, mc, :], pqk[:, T:2 * T], AF.Copy), inc="act")
            ev_cp[mc] = self.cnt["act"]
        EV[("zmm", t)] = self.cnt["pe"]
        EV[("cpz", t)] = ev_cp[KC - 1]
        eqb = self.sqb_g
        self.wait("pool", "dve", ev_zbd)
        self.emit("pool", lambda E: E.tensor_tensor(alf[:], lacum[:], zb[:],
                                                    ALU.subtract), inc="pool")
        self.wait("act", "pool")
        self.emit("act", lambda E, eqb=eqb: E.activation(eqb[:], alf[:], AF.Exp),
                  inc="act")
        ev_eq = self.cnt["act"]
        EV[("zfree", t)] = self.cnt["pool"]
        self.wait("pool", "act", ev_eq)
        for mc in range(KC):
            lastb = lacum[:, mc, CHL - 1::CHL].unsqueeze(2).broadcast_to(
                [128, CPT, CHL])
            self.emit("pool", lambda E, mc=mc, lastb=lastb: E.tensor_tensor(
                alf[:, mc, :].rearrange("p (c l) -> p c l", l=CHL), lastb,
                alf[:, mc, :].rearrange("p (c l) -> p c l", l=CHL), ALU.subtract),
                inc="pool" if mc == KC - 1 else None)
        EV[("alfp", t)] = self.cnt["pool"]
        self.wait("pool", "act", ev_cp[KC - 1])
        self.emit("pool", lambda E, qb=qb: E.tensor_tensor(qsim[:], qb[:], rqb[:],
                                                           ALU.mult))
        self.emit("pool", lambda E, kb=kb: E.tensor_tensor(ksim[:], kb[:], rkb[:],
                                                           ALU.mult), inc="pool")
        EV[("simin", t)] = self.cnt["pool"]
        self.emit("pool", lambda E, qb=qb, eqb=eqb: E.tensor_tensor(
            qb[:], qsim[:], eqb[:], ALU.mult), inc="pool")
        ev_qd = self.cnt["pool"]
        self.wait("act", "pool", ev_qd)
        self.emit("act", lambda E: E.activation(rqb[:], alf[:], AF.Exp), inc="act")
        self.wait("pool", "act")
        self.emit("pool", lambda E, kb=kb: E.tensor_tensor(kb[:], ksim[:], rqb[:],
                                                           ALU.mult), inc="pool")
        EV[("fact", t)] = self.cnt["pool"]

    def _A_stage2(self, t, L, EV):
        d = self.d
        T = TILE
        (qsim, ksim, cd, kdtm, v_tm, oT, kvsb) = (
            L["qsim"], L["ksim"], L["cdb"][t % 2], L["kdtm"],
            L["v_tm"], L["oT"], L["kvsb"])
        attnT = L["rkb"][:].rearrange("p c t -> p (c t)")
        qdec, kdec = L["qbb"][t % 2], L["kbb"][t % 2]
        xf = L["xfb"][t % 2]
        pj = L["pj"]
        psim, pso, pskv, pskt = L["psim"], L["pso"], L["pskv"], L["pskt"]
        wo = L["wo"]

        rowstate = {0: None, 64: None}

        def mmq(row, col, make, inc=None):
            if rowstate[col] is not None and rowstate[col] != row:
                self.emit("pe", lambda E: E.ldweights(self.ident_b[:]))
                rowstate[0] = rowstate[64] = None
            rowstate[col] = row
            return self.emit("pe", make, inc=inc)

        def full_k():
            rowstate[0] = rowstate[64] = None

        ST = {}

        def gla_head(ch):
            par = (ch % 2) * 64
            tok = slice(ch * CHL, (ch + 1) * CHL)
            st = ST[ch] = {}
            if ch == 0:
                self.wait("pe", "pool", EV[("simin", t)])
                self.wait("pe", "dve", EV[("zbd", t)])
                if ("attnc", t - 1) in EV:
                    self.wait("pe", "act", EV[("attnc", t - 1)])
            else:
                self.wait("pe", "act", ST[ch - 1]["attc"])
            ev_attc = None
            for hg in range(2):
                if hg == 1:
                    self.wait("pe", "act", ev_attc)
                hhs = [0, 2, 4, 6, 1, 3, 5, 7]
                for hh in hhs:
                    h = hg * 8 + hh
                    fb = (h % 2) * 64
                    kcq = h // 2
                    mmq(fb, par, lambda E, h=h, hh=hh, fb=fb, kcq=kcq, par=par,
                        tok=tok: E.matmul(
                        psim[par:par + 64, hh * 64:(hh + 1) * 64],
                        ksim[fb:fb + 64, kcq, tok], qsim[fb:fb + 64, kcq, tok],
                        start=(hh == 0), stop=(hh == 7), tile_position=(fb, par)),
                        inc="pe" if hh == 7 else None)
                ev_simg = self.cnt["pe"]
                if hg == 0:
                    if ch == 0:
                        self.wait("pe", "pool", EV[("fact", t)])
                        if ("kdtm", t - 1) in EV:
                            self.wait("pe", "act", EV[("kdtm", t - 1)])
                    else:
                        self.wait("pe", "act", ST[ch - 1]["kdtm"])
                    for kc in range(KC):
                        self.emit("pe", lambda E, kc=kc, tok=tok: E.transpose(
                            pskt[0:64, kc * 128:(kc + 1) * 128], kdec[:, kc, tok],
                            self.ident_b[:]), inc="pe" if kc == KC - 1 else None)
                    full_k()
                    st["kt"] = self.cnt["pe"]
                self.wait("act", "pe", ev_simg)
                self.emit("act", lambda E, par=par, hg=hg: E.copy(
                    attnT[par:par + 64, hg * 512:(hg + 1) * 512],
                    psim[par:par + 64, :]), inc="act")
                ev_attc = self.cnt["act"]
                self.wait("pool", "act", ev_attc)
                self.emit("pool", lambda E, par=par, hg=hg: E.tensor_tensor(
                    attnT[par:par + 64, hg * 512:(hg + 1) * 512].rearrange(
                        "p (h c) -> p h c", h=8),
                    attnT[par:par + 64, hg * 512:(hg + 1) * 512].rearrange(
                        "p (h c) -> p h c", h=8),
                    self.triu_b[par:par + 64, :].unsqueeze(1).broadcast_to(
                        [64, 8, 64]), ALU.mult), inc="pool")
            st["attc"] = ev_attc
            st["attn"] = self.cnt["pool"]
            self.wait("act", "pe", st["kt"])
            self.emit("act", lambda E, par=par: E.copy(kdtm[par:par + 64, :],
                                                       pskt[0:64, :]), inc="act")
            st["kdtm"] = self.cnt["act"]

        def gla_tail(ch):
            par = (ch % 2) * 64
            sub = ch // 2
            tok = slice(ch * CHL, (ch + 1) * CHL)
            st = ST[ch]
            if ch >= 1:
                self.wait("pe", "pool", ST[ch - 1]["supd"])
            else:
                self.wait("pe", "pool", st["attn"])
            self.wait("pe", "act", EV[("vtm", t)])
            if ch == 0:
                if ("apl", t + 1) in EV:
                    self.wait("pe", "dve", EV[("apl", t + 1)])
                else:
                    self.wait("pe", "dve")
                if ("oTev", t - 1) in EV:
                    self.wait("pe", "act", EV[("oTev", t - 1)])

            else:
                self.wait("pe", "act", ST[ch - 1]["oTe"])
            for parity in range(2):
                fb = parity * 64
                for pair in range(KC):
                    mmq(fb, fb, lambda E, fb=fb, pair=pair, tok=tok: E.matmul(
                        pso[fb:fb + 64, pair * 64:(pair + 1) * 64],
                        self.S_b[fb:fb + 64, pair * 64:(pair + 1) * 64],
                        qdec[fb:fb + 64, pair, tok],
                        start=(pair == 0), stop=False, tile_position=(fb, fb)))
                for pair in range(KC):
                    h = 2 * pair + parity
                    mmq(par, fb, lambda E, h=h, fb=fb, pair=pair, par=par, sub=sub:
                        E.matmul(
                        pso[fb:fb + 64, pair * 64:(pair + 1) * 64],
                        v_tm[par:par + 64, sub, h * 64:(h + 1) * 64],
                        attnT[par:par + 64, h * 64:(h + 1) * 64],
                        start=False, stop=(pair == KC - 1), tile_position=(par, fb)),
                        inc="pe" if (parity == 1 and pair == KC - 1) else None)
            ev_omm = self.cnt["pe"]
            self.wait("pe", "act", st["kdtm"])
            if ch >= 1:
                self.wait("pe", "act", ST[ch - 1]["kvc"])
            for parity in range(2):
                fb = parity * 64
                for pair in range(KC):
                    h = 2 * pair + parity
                    mmq(par, fb, lambda E, h=h, fb=fb, pair=pair, par=par, sub=sub:
                        E.matmul(
                        pskv[fb:fb + 64, pair * 64:(pair + 1) * 64],
                        kdtm[par:par + 64, pair * 128 + fb:pair * 128 + fb + 64],
                        v_tm[par:par + 64, sub, h * 64:(h + 1) * 64],
                        start=(pair == 0), stop=(pair == KC - 1),
                        tile_position=(par, fb)),
                        inc="pe" if (parity == 1 and pair == KC - 1) else None)
            ev_kv = self.cnt["pe"]
            self.wait("act", "pe", ev_omm)
            self.emit("act", lambda E, tok=tok: E.copy(
                oT[:, :, tok], pso[:].rearrange("p (k e) -> p k e", e=64)), inc="act")
            st["oTe"] = self.cnt["act"]
            self.wait("act", "pe", ev_kv)
            self.emit("act", lambda E: E.copy(kvsb[:], pskv[:]), inc="act")
            st["kvc"] = self.cnt["act"]
            self.wait("pool", "act", st["kvc"])
            if ch == 0:
                self.wait("pool", "act", EV[("cd", t)])
            cdap = cd[:, :, ch].unsqueeze(2).broadcast_to([128, KC, CHL])
            self.emit("pool", lambda E, cdap=cdap: E.tensor_tensor(
                self.S_f[:].rearrange("p (k e) -> p k e", e=64),
                self.S_f[:].rearrange("p (k e) -> p k e", e=64), cdap, ALU.mult))
            self.emit("pool", lambda E, kvsb=kvsb: E.tensor_tensor(
                self.S_f[:], self.S_f[:], kvsb[:], ALU.add))
            self.emit("pool", lambda E: E.tensor_copy(self.S_b[:], self.S_f[:]),
                      inc="pool")
            st["supd"] = self.cnt["pool"]

        for ch in range(CPT):
            gla_head(ch)
            if ch >= 1:
                gla_tail(ch - 1)
        gla_tail(CPT - 1)
        ev_kdtm = ST[CPT - 1]["kdtm"]
        ev_oTe = ST[CPT - 1]["oTe"]
        ev_kvc = ST[CPT - 1]["kvc"]
        EV[("attn", t)] = self.cnt["pool"]
        EV[("attnc", t)] = ev_kvc
        EV[("kdtm", t)] = ev_kdtm
        EV[("oTev", t)] = ev_oTe
        ev_oT = self.cnt["act"]

        ev_wo = {}
        self.wait("dve", "qs")
        if ("apl", t + 1) in EV:
            self.wait("pe", "dve", EV[("apl", t + 1)])
        else:
            self.wait("pe", "dve")
        for mc in range(KC):
            pp = pj[mc % 2]
            self.wait("pe", "act", ev_oT)
            if mc >= 2:
                self.wait("pe", "dve", ev_wo[mc - 2])
            for kc in range(KC):
                self.emit("pe", lambda E, kc=kc, mc=mc, pp=pp: E.matmul(
                    pp[:, 0:T], wo[:, kc, mc * 128:(mc + 1) * 128], oT[:, kc, :],
                    start=(kc == 0), stop=(kc == KC - 1)),
                    inc="pe" if kc == KC - 1 else None)
            self.wait("dve", "pe")
            self.emit("dve", lambda E, mc=mc, pp=pp, xf=xf: E.tensor_tensor(
                xf[:, mc, :], pp[:, 0:T], xf[:, mc, :], ALU.add), inc="dve")
            ev_wo[mc] = self.cnt["dve"]
            if mc == KC // 2 - 1 or mc == KC - 1:
                h = slice(0, KC // 2) if mc < KC - 1 else slice(KC // 2, KC)
                q = "qw%d" % (t % 2) if mc < KC - 1 else "qs"
                self.wait("sync", "dve", ev_wo[mc])
                self.dma("sync", lambda E, xf=xf, t=t, h=h: E.dma_start(
                    out=d["xa_s"][t][:, h, :], in_=xf[:, h, :]), q)
                EV[("woq", t, h.start)] = self.cnt[q]
        EV[("wo", t)] = self.cnt["dve"]

    def phaseB(self):
        self._b2st = {}
        d, v = self.d, self.vec
        T = TILE
        T2 = 2 * T
        nt2 = self.nt // 2
        with contextlib.ExitStack() as st:
            sb = lambda n, s_, dt: self.sb(n, s_, dt, st)
            ps = lambda n, s_, dt=F32: self.ps(n, s_, dt, st)
            w1 = sb("B_w1", [128, KC, FFN], BF16)
            load_weight(self, w1, "L0_attn_W1", KC)
            xfb = [sb(f"B_xf{i}", [128, KC, T], F32) for i in range(2)]
            hb = sb("B_hb", [128, KC, T2], BF16)
            u5 = sb("B_u5", [128, 32, T2], BF16)
            b1f = sb("B_b1f", [128, 32], F32)
            lnbb = sb("B_lnbb", [128, KC], BF16)
            pstat = ps("B_pstat", [16, 512])
            pj = [ps(f"B_pj{i}", [128, 512]) for i in range(3)]
            self.barrier()
            tmpf = self.tmpf_g
            r1 = self.row1
            ev_load = {0: self.dma("sync", lambda E: E.dma_start(
                out=xfb[0][:], in_=d["xa_s"][0]), "ql0")}
            self.emit("dve", lambda E: E.tensor_copy(lnbb[:], v["ln2b"][:]),
                      inc="dve")
            self.wait("pe", "dve")
            self.wait("pe", "qp")
            pjb = pj[2]
            for mh in range(32):
                for kc in range(KC):
                    self.emit("pe", lambda E, mh=mh, kc=kc, pjb=pjb: E.matmul(
                        pjb[:, mh:mh + 1], w1[:, kc, mh * 128:(mh + 1) * 128],
                        lnbb[:, kc:kc + 1], start=(kc == 0), stop=(kc == KC - 1)),
                        inc="pe" if (mh == 31 and kc == KC - 1) else None)
            self.wait("dve", "pe")
            self.emit("dve", lambda E, pjb=pjb: E.tensor_tensor(
                b1f[:], pjb[:, 0:32], v["b1"][:], ALU.add), inc="dve")
            self.wait("pe", "dve")
            ln_tile(self, pstat, xfb[0], double=False, qs_sem="ql0",
                    qs_val=ev_load[0])
            for t in range(self.nt):
                xf = xfb[t % 2]
                half = (t % 2) * T
                if t + 1 < self.nt:
                    self.wait("sync", "dve")
                    self.wait("sync", "qs")
                    ev_load[t + 1] = self.dma("sync", lambda E, t=t: E.dma_start(
                        out=xfb[(t + 1) % 2][:], in_=d["xa_s"][t + 1]),
                        "ql%d" % ((t + 1) % 2))
                evA = bcast(self, pj[0], r1(LN["A"]))
                evB = bcast(self, pj[1], r1(LN["B"]))
                self.wait("dve", "pe", evB)
                self.wait("dve", "pool")
                bA = pj[0][:, 0:T].unsqueeze(1).broadcast_to([128, KC, T])
                bB = pj[1][:, 0:T].unsqueeze(1).broadcast_to([128, KC, T])
                self.emit("dve", lambda E, xf=xf, bA=bA: E.tensor_tensor(
                    tmpf[:], xf[:], bA, ALU.mult))
                self.emit("dve", lambda E, bB=bB: E.tensor_tensor(
                    tmpf[:], tmpf[:], bB, ALU.subtract), inc="dve")
                self.wait("dve", "pe")
                hbh = hb[:].rearrange("p c (h n) -> p c h n", n=T)[
                    :, :, half // T, :]
                for kc in range(KC):
                    self.emit("dve", lambda E, kc=kc, hbh=hbh: E.tensor_scalar(
                        hbh[:, kc, :], tmpf[:, kc, :], v["ln2w"][:, kc:kc + 1],
                        None, ALU.mult), inc="dve" if kc == KC - 1 else None)
                ev_h = self.cnt["dve"]
                if t + 1 < self.nt:
                    ln_tile(self, pstat, xfb[(t + 1) % 2], double=False,
                            qs_sem="ql%d" % ((t + 1) % 2), qs_val=ev_load[t + 1])
                if t % 2 == 1:
                    ev_gelu = {}
                    for mh in range(32):
                        pp = pj[mh % 3]
                        if mh >= 3:
                            self.wait("pe", "act", ev_gelu[mh - 3])
                        self.wait("pe", "dve", ev_h)
                        if mh == 0:
                            self.wait("pe", "qs")
                        for kc in range(KC):
                            self.emit("pe", lambda E, kc=kc, mh=mh, pp=pp: E.matmul(
                                pp[:], w1[:, kc, mh * 128:(mh + 1) * 128],
                                hb[:, kc, :], start=(kc == 0), stop=(kc == KC - 1)),
                                inc="pe" if kc == KC - 1 else None)
                        evmm = self.cnt["pe"]
                        self.wait("act", "pe", evmm)
                        self.emit("act", lambda E, mh=mh, pp=pp: E.activation(
                            u5[:, mh, :], pp[:], AF.Gelu,
                            bias=b1f[:, mh:mh + 1]), inc="act")
                        ev_gelu[mh] = self.cnt["act"]
                    self.wait("sync", "act", ev_gelu[31])
                    self.dma("sync", lambda E, t=t: E.dma_start(
                        out=d["u_s"][t // 2], in_=u5[:]), "qs")
        self.barrier()
        with contextlib.ExitStack() as st:
            sb = lambda n, s_, dt: self.sb(n, s_, dt, st)
            ps = lambda n, s_, dt=F32: self.ps(n, s_, dt, st)
            w2 = sb("B2_w2", [128, 32, 1024], BF16)
            load_weight(self, w2, "L0_attn_W2", 32)
            u5b = [sb(f"B2_u{i}", [128, 32, T2], BF16) for i in range(2)]
            xf5 = sb("B2_xf", [128, KC, T2], F32)
            t15 = [sb(f"B2_t1{i}", [128, T2], F32) for i in range(2)]
            pj = [ps(f"B2_pj{i}", [128, 512]) for i in range(2)]
            self.barrier()

            def xa_pair(stt):
                return d["xa_s"][2 * stt:2 * stt + 2].rearrange(
                    "t p c n -> p c t n")

            def v4(buf):
                return buf[:].rearrange("p c (t n) -> p c t n", t=2)

            def s_half(stt, mc):
                return d["s_s"][2 * stt:2 * stt + 2, :, mc, :].rearrange(
                    "t p n -> p t n")

            ev_u = {0: self.dma("sync", lambda E: E.dma_start(
                out=u5b[0][:], in_=d["u_s"][0]), "ql0")}
            ev_x = {0: self.dma("sync", lambda E: E.dma_start(
                out=v4(xf5), in_=xa_pair(0)), "qs")}
            for stt in range(nt2):
                u5c = u5b[stt % 2]
                if stt + 1 < nt2:
                    self.wait("sync", "pe")
                    ev_u[stt + 1] = self.dma("sync", lambda E, stt=stt: E.dma_start(
                        out=u5b[(stt + 1) % 2][:], in_=d["u_s"][stt + 1]),
                        "ql%d" % ((stt + 1) % 2))
                self.wait("pe", "ql%d" % (stt % 2), ev_u[stt])
                if stt >= 1:
                    self.wait("pe", "dve", prev_last)
                ev_ev = {}
                for mc in range(KC):
                    pp = pj[mc % 2]
                    if mc >= 2:
                        self.wait("pe", "dve", ev_ev[mc - 2])
                    for mh in range(32):
                        self.emit("pe", lambda E, mh=mh, mc=mc, pp=pp, u5c=u5c:
                                  E.matmul(
                            pp[:], w2[:, mh, mc * 128:(mc + 1) * 128], u5c[:, mh, :],
                            start=(mh == 0), stop=(mh == 31)),
                            inc="pe" if mh == 31 else None)
                    evmm = self.cnt["pe"]
                    self.wait("dve", "pe", evmm)
                    if mc == 0:
                        self.wait("dve", "qs")
                    ev_st = getattr(self, "_b2st", {})
                    if (stt * KC + mc) >= 2:
                        self.wait("dve", "qt%d" % (mc % 2),
                                  ev_st[(stt * KC + mc - 2)])
                    self.emit("dve", lambda E, mc=mc, pp=pp: E.tensor_scalar(
                        t15[mc % 2][:], pp[:], v["b2"][:, mc:mc + 1], None, ALU.add))
                    self.emit("dve", lambda E, mc=mc: E.tensor_tensor(
                        t15[mc % 2][:], t15[mc % 2][:], xf5[:, mc, :], ALU.add),
                        inc="dve")
                    ev_ev[mc] = self.cnt["dve"]
                    self.wait("sync", "dve", ev_ev[mc])
                    ev_st[stt * KC + mc] = self.dma(
                        "sync", lambda E, stt=stt, mc=mc: E.dma_start(
                            out=s_half(stt, mc),
                            in_=t15[mc % 2][:].rearrange("p (t n) -> p t n", t=2)),
                        "qt%d" % (mc % 2))
                    self._b2st = ev_st
                prev_last = ev_ev[KC - 1]
                if stt + 1 < nt2:
                    ev_x[stt + 1] = self.dma("sync", lambda E, stt=stt: E.dma_start(
                        out=v4(xf5), in_=xa_pair(stt + 1)), "qs")

    def _ffn_up(self, L):
        P = L + "_"
        d = self.d
        T = TILE
        T2 = 2 * T
        nt2 = self.nt // 2
        NH = HID3 // 128
        with contextlib.ExitStack() as st:
            wg = self.sb(P + "F_wg", [128, KC, HID3], BF16, st)
            wu = self.sb(P + "F_wu", [128, KC, HID3], BF16, st)
            load_weight(self, wg, f"{L}_wgT", KC)
            load_weight(self, wu, f"{L}_wuT", KC)
            midb = self.sb(P + "F_mid", [128, NH, T2], BF16, st)
            silu_t = [self.sb(P + f"F_sl{i}", [128, T2], BF16, st) for i in range(3)]
            xnbb = [self.sb(P + f"F_xn{i}", [128, KC, T2], BF16, st) for i in range(2)]
            pj = [self.ps(P + f"F_pj{i}", [128, 512], F32, st) for i in range(6)]
            self.barrier()
            src = d["xn_s"] if L == "L0" else d["xn1_s"]

            def src_pair(stt):
                return src[2 * stt:2 * stt + 2].rearrange("t p c n -> p c t n")

            def mid_pair(stt):
                return d["mid_s"][2 * stt:2 * stt + 2].rearrange(
                    "t p c n -> p c t n")

            def v4(buf):
                return buf[:].rearrange("p c (t n) -> p c t n", t=2)

            ev_ld = {0: self.dma("pool", lambda E: E.dma_start(
                out=v4(xnbb[0]), in_=src_pair(0)), "qp0")}
            for stt in range(nt2):
                xnb = xnbb[stt % 2]
                if stt + 1 < nt2:
                    self.wait("pool", "pe")
                    ev_ld[stt + 1] = self.dma("pool", lambda E, stt=stt: E.dma_start(
                        out=v4(xnbb[(stt + 1) % 2]), in_=src_pair(stt + 1)),
                        "qp%d" % ((stt + 1) % 2))
                self.wait("pe", "qp%d" % (stt % 2), ev_ld[stt])
                self.wait("pe", "dve")
                self.wait("pe", "act")
                self.wait("dve", "qs")
                ev_mid = {}
                for mh in range(NH):
                    pg, pu = pj[2 * (mh % 3)], pj[2 * (mh % 3) + 1]
                    if mh >= 3:
                        self.wait("pe", "dve", ev_mid[mh - 3])
                    for kc in range(KC):
                        self.emit("pe", lambda E, kc=kc, mh=mh, pg=pg, xnb=xnb:
                                  E.matmul(
                            pg[:], wg[:, kc, mh * 128:(mh + 1) * 128], xnb[:, kc, :],
                            start=(kc == 0), stop=(kc == KC - 1)),
                            inc="pe" if kc == KC - 1 else None)
                    ev_g = self.cnt["pe"]
                    for kc in range(KC):
                        self.emit("pe", lambda E, kc=kc, mh=mh, pu=pu, xnb=xnb:
                                  E.matmul(
                            pu[:], wu[:, kc, mh * 128:(mh + 1) * 128], xnb[:, kc, :],
                            start=(kc == 0), stop=(kc == KC - 1)),
                            inc="pe" if kc == KC - 1 else None)
                    ev_u = self.cnt["pe"]
                    self.wait("act", "pe", ev_g)
                    if mh >= 3:
                        self.wait("act", "dve", ev_mid[mh - 3])
                    self.emit("act", lambda E, mh=mh, pg=pg: E.activation(
                        silu_t[mh % 3][:], pg[:], AF.Silu, scale=1.0 / 127.0),
                        inc="act")
                    self.wait("dve", "pe", ev_u)
                    self.wait("dve", "act")
                    self.emit("dve", lambda E, mh=mh, pu=pu: E.scalar_tensor_tensor(
                        midb[:, mh, :], pu[:], 1.0 / 127.0, silu_t[mh % 3][:],
                        ALU.mult, ALU.mult))
                    ev_mid[mh] = self.cnt["dve"]
                self.wait("sync", "dve")
                self.dma("sync", lambda E, stt=stt: E.dma_start(
                    out=mid_pair(stt),
                    in_=midb[:].rearrange("p c (t n) -> p c t n", t=2)), "qs")

    def _ffn_down(self, L, xsrc, ssrc, dst):
        P = L + "_"
        d = self.d
        T = TILE
        T2 = 2 * T
        nt2 = self.nt // 2
        NH = HID3 // 128
        with contextlib.ExitStack() as st:
            wd = self.sb(P + "F_wd", [128, NH, 1024], BF16, st)
            load_weight(self, wd, f"{L}_wdT", NH)
            midbb = [self.sb(P + f"F_mid2{i}", [128, NH, T2], BF16, st)
                     for i in range(2)]
            xf5 = self.sb(P + "F_xf", [128, KC, T2], F32, st)
            sf5 = self.sb(P + "F_sf", [128, KC, T2], F32, st)
            gt = [self.sb(P + f"F_gt{i}", [128, T2], F32, st) for i in range(2)]
            pj = [self.ps(P + f"F2_pj{i}", [128, 512], F32, st) for i in range(2)]
            self.barrier()
            tmp2 = [self.sb(P + f"F_tm{i}", [128, T2], F32, st) for i in range(2)]

            def pair(name, stt):
                return d[name][2 * stt:2 * stt + 2].rearrange("t p c n -> p c t n")

            def v4(buf):
                return buf[:].rearrange("p c (t n) -> p c t n", t=2)

            def dst_half(name, stt, mc):
                return d[name][2 * stt:2 * stt + 2, :, mc, :].rearrange(
                    "t p n -> p t n")

            ev_ld = {0: self.dma("sync", lambda E: E.dma_start(
                out=v4(midbb[0]), in_=pair("mid_s", 0)), "ql0")}
            self.dma("sync", lambda E: E.dma_start(out=v4(xf5), in_=pair(xsrc, 0)),
                     "qs")
            ev_xs = {0: self.dma("sync", lambda E: E.dma_start(
                out=v4(sf5), in_=pair(ssrc, 0)), "qs")}
            for stt in range(nt2):
                midb = midbb[stt % 2]
                if stt + 1 < nt2:
                    self.wait("sync", "pe")
                    ev_ld[stt + 1] = self.dma("sync", lambda E, stt=stt: E.dma_start(
                        out=v4(midbb[(stt + 1) % 2]), in_=pair("mid_s", stt + 1)),
                        "ql%d" % ((stt + 1) % 2))
                self.wait("pe", "ql%d" % (stt % 2), ev_ld[stt])
                self.wait("pe", "act")
                self.wait("act", "dve")
                ev_c = {}
                ev_mul = {}
                for mc in range(KC):
                    pp = pj[mc % 2]
                    if mc >= 2:
                        self.wait("pe", "act", ev_c[mc - 2])
                    for mh in range(NH):
                        self.emit("pe", lambda E, mh=mh, mc=mc, pp=pp, midb=midb:
                                  E.matmul(
                            pp[:], wd[:, mh, mc * 128:(mc + 1) * 128],
                            midb[:, mh, :], start=(mh == 0), stop=(mh == NH - 1)),
                            inc="pe" if mh == NH - 1 else None)
                    ev_mm = self.cnt["pe"]
                    self.wait("act", "pe", ev_mm)
                    if mc >= 2:
                        self.wait("act", "dve", ev_mul[mc - 2])
                    self.emit("act", lambda E, mc=mc, pp=pp: E.activation(
                        gt[mc % 2][:], pp[:], AF.Silu, scale=1.0 / 127.0),
                        inc="act")
                    ev_c[mc] = self.cnt["act"]
                    self.wait("dve", "act", ev_c[mc])
                    if mc == 0:
                        self.wait("dve", "qs", ev_xs[stt])
                    ev_st = getattr(self, "_fdst", {})
                    key = stt * KC + mc
                    if key >= 2:
                        self.wait("dve", "qt%d" % (mc % 2), ev_st[key - 2])
                    self.emit("dve", lambda E, mc=mc: E.tensor_tensor(
                        tmp2[mc % 2][:], sf5[:, mc, :], gt[mc % 2][:], ALU.mult))
                    ev_mul[mc] = self.cnt["dve"]
                    self.emit("dve", lambda E, mc=mc: E.tensor_tensor(
                        tmp2[mc % 2][:], tmp2[mc % 2][:], xf5[:, mc, :], ALU.add),
                        inc="dve")
                    self.wait("sync", "dve")
                    ev_st[key] = self.dma(
                        "sync", lambda E, stt=stt, mc=mc, dst=dst: E.dma_start(
                            out=dst_half(dst, stt, mc),
                            in_=tmp2[mc % 2][:].rearrange("p (t n) -> p t n", t=2)),
                        "qt%d" % (mc % 2))
                    self._fdst = ev_st
                if stt + 1 < nt2:
                    self.dma("sync", lambda E, stt=stt, xsrc=xsrc: E.dma_start(
                        out=v4(xf5), in_=pair(xsrc, stt + 1)), "qs")
                    ev_xs[stt + 1] = self.dma("sync", lambda E, stt=stt, ssrc=ssrc:
                                              E.dma_start(
                        out=v4(sf5), in_=pair(ssrc, stt + 1)), "qs")

    def phaseC(self):
        self._fdst = {}
        _ffn_up(self, "L0")
        self.barrier()
        _ffn_down(self, "L0", "x_in", "s_s", "x1_s")

    def phaseE(self):
        self._fdst = {}
        _ffn_up(self, "L1")
        self.barrier()
        _ffn_down(self, "L1", "x1_s", "s1_s", "out")

    def phaseD(self):
        d, v = self.d, self.vec
        T = TILE
        NE = HID2 // 128
        with contextlib.ExitStack() as st:
            sb = lambda n, s, dt: self.sb(n, s, dt, st)
            ps = lambda n, s, dt=F32: self.ps(n, s, dt, st)
            wh = sb("D_wh", [128, KC, 4096], BF16)
            wqk = sb("D_wqk", [128, KC, QKD], BF16)
            wout = sb("D_wout", [128, NE, 1024], BF16)
            load_weight(self, wh, "L1_attn_Wh", KC)
            load_weight(self, wqk, "L1_attn_Wqk", KC)
            load_weight(self, wout, "L1_attn_Wout", NE)
            x1fb = [sb(f"D_x1f{i}", [128, KC, T], F32) for i in range(2)]
            normb = sb("D_normb", [128, KC, T], BF16)
            hvb = sb("D_hvb", [128, 32, T], BF16)
            qkf = sb("D_qkf", [128, T], F32)
            qvar = [dict(qq=sb(f"D_qq{i}", [128, T], BF16),
                         qkk=sb(f"D_qkk{i}", [128, T], BF16),
                         lq=sb(f"D_lq{i}", [128, T], BF16),
                         lk=sb(f"D_lk{i}", [128, T], BF16)) for i in range(2)]
            v_tm = sb("D_vtm", [128, 2, HID2], BF16)
            attnT = sb("D_attnT", [128, 2, T], BF16)
            lkT = sb("D_lkT", [128, 2, 128], BF16)
            t1 = sb("D_t1", [128, T], F32)
            self.linkv_f = sb("linkv_f", [128, HID2], F32)
            self.linkv_b = sb("linkv_b", [128, HID2], BF16)
            self.emit("pool", lambda E: E.memset(self.linkv_f[:], 0.0), inc="pool")
            self.emit("pool", lambda E: E.memset(self.linkv_b[:], 0.0), inc="pool")
            pstat = ps("D_pstat", [16, 512])
            pj = [ps(f"D_pj{i}", [128, 512]) for i in range(2)]
            psm = ps("D_psm", [128, 512])
            pvt = ps("D_pvt", [128, 2048], BF16)
            pkv = ps("D_pkv", [128, 1024])
            preb = hvb[:, 0:NE, :]
            bh2 = sb("D_bh2", [128, 32], F32)
            bqk2 = sb("D_bqk2", [128, 1], F32)
            lnbb = sb("D_lnbb", [128, KC], BF16)
            biasT_l = sb("D_biasT", [128, 2, 256], BF16)
            self.dma("pool", lambda E, biasT_l=biasT_l: E.dma_start(
                out=biasT_l[:], in_=d["biasT_d"]), "qp")
            self.barrier()
            tmpf = self.tmpf_g
            r1 = self.row1
            ev_load = {0: self.dma("sync", lambda E: E.dma_start(
                out=x1fb[0][:], in_=d["x1_s"][0]), "ql0")}
            self.emit("dve", lambda E: E.tensor_copy(lnbb[:], v["lnb"][:]),
                      inc="dve")
            self.wait("pe", "dve")
            self.wait("pe", "qp")
            for mh in range(32):
                for kc in range(KC):
                    self.emit("pe", lambda E, mh=mh, kc=kc: E.matmul(
                        pj[0][:, mh:mh + 1], wh[:, kc, mh * 128:(mh + 1) * 128],
                        lnbb[:, kc:kc + 1], start=(kc == 0), stop=(kc == KC - 1)),
                        inc="pe" if (mh == 31 and kc == KC - 1) else None)
            for kc in range(KC):
                self.emit("pe", lambda E, kc=kc: E.matmul(
                    pj[1][:, 0:1], wqk[:, kc, :], lnbb[:, kc:kc + 1],
                    start=(kc == 0), stop=(kc == KC - 1)),
                    inc="pe")
            self.wait("dve", "pe")
            self.emit("dve", lambda E: E.tensor_tensor(
                bh2[:], pj[0][:, 0:32], v["bh"][:], ALU.add))
            self.emit("dve", lambda E: E.tensor_tensor(
                bqk2[:], pj[1][:, 0:1], v["bqk"][:], ALU.add), inc="dve")
            self.wait("pe", "dve")
            L = locals()
            EV = {}
            ln_tile(self, pstat, x1fb[0], double=True, qs_sem="ql0",
                    qs_val=ev_load[0])
            for t in range(self.nt):
                _D_s1a(self, t, L, EV)
                if t >= 1:
                    _D_s2(self, t - 1, L, EV)
                _D_s1b(self, t, L, EV)
                if t + 1 < self.nt:
                    h1, h2 = slice(0, KC // 2), slice(KC // 2, KC)
                    self.wait("sync", "dve", EV.get(("s1h", t - 1)))
                    if t >= 1:
                        self.wait("sync", "qx%d" % ((t - 1) % 2),
                                  EV[("xn1q", t - 1)])
                    ev_l1 = self.dma("sync", lambda E, t=t, h1=h1: E.dma_start(
                        out=x1fb[(t + 1) % 2][:, h1, :],
                        in_=d["x1_s"][t + 1][:, h1, :]), "qh%d" % ((t + 1) % 2))
                    self.wait("sync", "dve", EV.get(("s1d", t - 1)))
                    ev_load[t + 1] = self.dma("sync", lambda E, t=t, h2=h2:
                                              E.dma_start(
                        out=x1fb[(t + 1) % 2][:, h2, :],
                        in_=d["x1_s"][t + 1][:, h2, :]), "ql%d" % ((t + 1) % 2))
                    ln_tile(self, pstat, x1fb[(t + 1) % 2], double=True,
                            qs_sem="ql%d" % ((t + 1) % 2), qs_val=ev_load[t + 1],
                            eng="pool", pe_guard=EV.get(("lnpe", t)),
                            half_vals=(ev_l1, ev_load[t + 1]))
                    EV[("lnpe", t + 1)] = self._ln_pe
                    EV[("lnrows", t + 1)] = self._ln_ev[1]
                    EV[("lnAB", t + 1)] = self._ln_evAB[1]
                    EV[("lnABd", t + 1)] = self._ln_evAB[0]
            _D_s2(self, self.nt - 1, L, EV)

    def _D_s1a(self, t, L, EV):
        d, v = self.d, self.vec
        T = TILE
        r1 = self.row1
        x1f = L["x1fb"][t % 2]
        pj, psm = L["pj"], L["psm"]
        tmpf = self.tmpf_g
        lnr = EV.get(("lnrows", t))
        lnab = EV.get(("lnAB", t))
        evA = bcast(self, pj[0], r1(LN["A"]), rbf_pe=EV.get(("evR", t - 1)),
                    ps_act=EV.get(("hv", t - 1)),
                    ps_dve=EV.get(("s1d", t - 2), EV.get(("apl", t - 1))),
                    row_act=lnab)
        evB = bcast(self, pj[1], r1(LN["B"]), rbf_pe=evA,
                    ps_act=EV.get(("hv", t - 1)),
                    ps_dve=EV.get(("s1d", t - 2), EV.get(("apl", t - 1))),
                    row_act=lnab)
        evR = bcast(self, L["pkv"], r1(LN["R2"]), rbf_pe=evB,
                    ps_act=EV.get(("lnrows", t)),
                    ps_dve=EV.get(("s1d", t - 1), EV.get(("apl", t - 1))),
                    row_act=lnr)
        EV[("evR", t)] = evR
        self.wait("dve", "pe", evB)
        self.wait("dve", "qs")
        bA = pj[0][:, 0:T].unsqueeze(1).broadcast_to([128, KC, T])
        bB = pj[1][:, 0:T].unsqueeze(1).broadcast_to([128, KC, T])
        self.emit("dve", lambda E, x1f=x1f, bA=bA: E.tensor_tensor(
            tmpf[:], x1f[:], bA, ALU.mult))
        self.emit("dve", lambda E, x1f=x1f, bB=bB: E.tensor_tensor(
            x1f[:], tmpf[:], bB, ALU.subtract))
        ev_xn1 = self.cnt["dve"]
        self.wait("sync", "dve", ev_xn1)
        self.dma("sync", lambda E, t=t, x1f=x1f: E.dma_start(
            out=d["xn1_s"][t], in_=x1f[:]), "qx%d" % (t % 2))
        EV[("xn1q", t)] = self.cnt["qx%d" % (t % 2)]
        self.wait("dve", "pe", evR)
        if ("qkm", t - 1) in EV:
            self.wait("dve", "pe", EV[("qkm", t - 1)])
        normb = L["normb"]
        for kc in range(KC):
            self.emit("dve", lambda E, kc=kc, x1f=x1f: E.scalar_tensor_tensor(
                normb[:, kc, :], x1f[:, kc, :], v["lnw"][:, kc:kc + 1],
                L["pkv"][:, 0:T], ALU.mult, ALU.mult),
                inc="dve" if kc == KC - 1 else None)
        ev_tm = self.cnt["dve"]
        EV[("apl", t)] = ev_tm
        EV[("nm", t)] = ev_tm

    def _D_s1b(self, t, L, EV):
        d, v = self.d, self.vec
        T = TILE
        normb, hvb, pj, psm = L["normb"], L["hvb"], L["pj"], L["psm"]
        wh, wqk = L["wh"], L["wqk"]
        V = L["qvar"][t % 2]
        ev_nm = EV[("nm", t)]
        if ("s1d", t - 1) in EV:
            self.wait("pe", "dve", EV[("s1d", t - 1)])
        self.wait("pe", "dve", ev_nm)
        for kc in range(KC):
            self.emit("pe", lambda E, kc=kc: E.matmul(
                psm[:, 0:T], wqk[:, kc, :], normb[:, kc, :],
                start=(kc == 0), stop=(kc == KC - 1)),
                inc="pe" if kc == KC - 1 else None)
        qkf = L["qkf"]
        self.wait("act", "pe")
        if ("var", t - 1) in EV:
            self.wait("act", "pool", EV[("var", t - 1)])
        self.emit("act", lambda E, qkf=qkf: E.activation(
            qkf[:], psm[:, 0:T], AF.Silu, bias=L["bqk2"][:, 0:1]), inc="act")
        EV[("qkf", t)] = self.cnt["act"]
        self.wait("pool", "act", EV[("qkf", t)])
        if ("qlm", t - 2) in EV:
            self.wait("pool", "pe", EV[("qlm", t - 2)])
        else:
            self.wait("pool", "pe")
        for name, col in (("qq", 0), ("lq", 1), ("qkk", 2), ("lk", 3)):
            dst = V[name]
            wb = v["qkw"][:, col:col + 1].broadcast_to([128, T])
            bb2 = v["qkb"][:, col:col + 1].broadcast_to([128, T])
            self.emit("pool", lambda E, dst=dst, wb=wb, qkf=qkf: E.tensor_tensor(
                dst[:], qkf[:], wb, ALU.mult))
            self.emit("pool", lambda E, dst=dst, bb2=bb2: E.tensor_tensor(
                dst[:], dst[:], bb2, ALU.add))
        self.emit("pool", lambda E: E.engine_nop(), inc="pool")
        EV[("var", t)] = self.cnt["pool"]
        ev_hv = {}
        self.wait("pe", "dve", EV[("apl", t)])
        if ("hv", t - 1) in EV:
            self.wait("pe", "act", EV[("hv", t - 1)])
        else:
            self.wait("pe", "act")
        for mh in range(32):
            pp = pj[mh % 2]
            if mh >= 2:
                self.wait("pe", "act", ev_hv[mh - 2])
            self.wait("pe", "dve", ev_nm)
            for kc in range(KC):
                self.emit("pe", lambda E, kc=kc, mh=mh, pp=pp: E.matmul(
                    pp[:, 0:T], wh[:, kc, mh * 128:(mh + 1) * 128],
                    normb[:, kc, :], start=(kc == 0), stop=(kc == KC - 1)),
                    inc="pe" if kc == KC - 1 else None)
            ev_mm = self.cnt["pe"]
            self.wait("act", "pe", ev_mm)
            self.emit("act", lambda E, mh=mh, pp=pp: E.activation(
                hvb[:, mh, :], pp[:, 0:T], AF.Silu,
                bias=L["bh2"][:, mh:mh + 1]), inc="act")
            ev_hv[mh] = self.cnt["act"]
        EV[("hv", t)] = ev_hv[31]
        EV[("qkm", t)] = self.cnt["pe"]

    def _D_s2(self, t, L, EV):
        d, v = self.d, self.vec
        T = TILE
        NE = HID2 // 128
        normb, hvb, pj, psm = L["normb"], L["hvb"], L["pj"], L["psm"]
        pvt, pkv = L["pvt"], L["pkv"]
        attnT, lkT, t1, v_tm = L["attnT"], L["lkT"], L["t1"], L["v_tm"]
        preb = L["preb"]
        wout = L["wout"]
        x1f = L["x1fb"][t % 2]
        V = L["qvar"][t % 2]
        qq, qkk, lq, lk = V["qq"], V["qkk"], V["lq"], V["lk"]
        tmpf = self.tmpf_g
        self.wait("pe", "pool", EV[("var", t)])
        if ("s1d", t - 1) in EV:
            self.wait("pe", "dve", EV[("s1d", t - 1)])

        for jt in range(2):
            self.emit("pe", lambda E, jt=jt, qkk=qkk, qq=qq: E.matmul(
                psm[:, jt * T:(jt + 1) * T], qkk[:, jt * 128:(jt + 1) * 128],
                qq[:], start=True, stop=True), inc="pe")
        ev_sim = self.cnt["pe"]
        self.wait("dve", "pe", ev_sim)
        for jt in range(2):
            self.emit("dve", lambda E, jt=jt: E.scalar_tensor_tensor(
                t1[:], psm[:, jt * T:(jt + 1) * T], 1.0 / 256.0,
                L["biasT_l"][:, jt, :], ALU.mult, ALU.add))
            self.wait("act", "dve")
            self.emit("act", lambda E: E.activation(t1[:], t1[:], AF.Relu),
                      inc="act")
            self.emit("act", lambda E, jt=jt: E.activation(
                attnT[:, jt, :], t1[:], AF.Square), inc="act")
            self.wait("dve", "act")
        ev_attn = self.cnt["act"]
        EV[("stt", t)] = self.cnt["dve"]
        for jt in range(2):
            self.wait("pe", "act")
            for ec in range(NE):
                self.emit("pe", lambda E, jt=jt, ec=ec: E.transpose(
                    pvt[:, ec * 128:(ec + 1) * 128],
                    hvb[:, ec, jt * 128:(jt + 1) * 128], self.ident_b[:]),
                    inc="pe" if ec == NE - 1 else None)
            self.wait("act", "pe")
            self.emit("act", lambda E, jt=jt: E.copy(v_tm[:, jt, :], pvt[:]),
                      inc="act")
        self.wait("pe", "act")
        for jt in range(2):
            self.emit("pe", lambda E, jt=jt, lk=lk: E.transpose(
                pvt[:, jt * 128:(jt + 1) * 128], lk[:, jt * 128:(jt + 1) * 128],
                self.ident_b[:]), inc="pe" if jt == 1 else None)
        self.wait("act", "pe")
        self.emit("act", lambda E: E.copy(
            lkT[:], pvt[:, 0:256].rearrange("p (j e) -> p j e", j=2)),
            inc="act")
        ev_lkT = self.cnt["act"]
        ev_pre = {}
        for ec in range(NE):
            pq = pj[ec % 2]
            if ec >= 2:
                self.wait("pe", "dve", ev_pre[ec - 2])
            self.wait("pe", "act", ev_attn)
            for jt in range(2):
                self.emit("pe", lambda E, ec=ec, jt=jt, pq=pq: E.matmul(
                    pq[:, 0:T], v_tm[:, jt, ec * 128:(ec + 1) * 128],
                    attnT[:, jt, :], start=(jt == 0), stop=False))
            self.emit("pe", lambda E, ec=ec, pq=pq, lq=lq: E.matmul(
                pq[:, 0:T], self.linkv_b[:, ec * 128:(ec + 1) * 128], lq[:],
                start=False, stop=True), inc="pe")
            ev_q = self.cnt["pe"]
            self.wait("dve", "pe", ev_q)
            self.emit("dve", lambda E, ec=ec, pq=pq: E.tensor_tensor(
                preb[:, ec, :], pq[:, 0:T], hvb[:, 16 + ec, :], ALU.mult))
            ev_pre[ec] = self.cnt["dve"]
        ev_preb = self.cnt["dve"]
        self.wait("pe", "act", ev_lkT)
        self.wait("pe", "dve", ev_preb)
        EV[("qlm", t)] = self.cnt["pe"]
        for eh in range(2):
            for sl in range(2):
                for jt in range(2):
                    self.emit("pe", lambda E, eh=eh, sl=sl, jt=jt: E.matmul(
                        pkv[:, sl * 512:(sl + 1) * 512],
                        lkT[:, jt, :],
                        v_tm[:, jt, eh * 1024 + sl * 512:eh * 1024 + (sl + 1) * 512],
                        start=(jt == 0), stop=(jt == 1)),
                        inc="pe" if (sl == 1 and jt == 1) else None)
            ev_kv = self.cnt["pe"]
            self.wait("dve", "pe", ev_kv)
            half = slice(eh * 1024, (eh + 1) * 1024)
            self.emit("dve", lambda E, half=half: E.scalar_tensor_tensor(
                self.linkv_f[:, half], pkv[:], 1.0 / 256.0,
                self.linkv_f[:, half], ALU.mult, ALU.add))
            self.emit("dve", lambda E, half=half: E.tensor_copy(
                self.linkv_b[:, half], self.linkv_f[:, half]))
            ev_lkv = self.cnt["dve"]
            EV[("lkv", t)] = ev_lkv
            if eh == 0:
                self.wait("pe", "dve", ev_lkv)
        ev_s1 = {}
        self.wait("pe", "dve")
        self.wait("dve", "qs")
        for mc in range(KC):
            pp = psm if mc % 2 == 0 else L["pkv"]
            if mc >= 2:
                self.wait("pe", "dve", ev_s1[mc - 2])
            for eh2 in range(NE):
                self.emit("pe", lambda E, eh2=eh2, mc=mc, pp=pp: E.matmul(
                    pp[:, 0:T], wout[:, eh2, mc * 128:(mc + 1) * 128],
                    preb[:, eh2, :], start=(eh2 == 0), stop=(eh2 == NE - 1)),
                    inc="pe" if eh2 == NE - 1 else None)
            ev_mm = self.cnt["pe"]
            self.wait("dve", "pe", ev_mm)
            self.emit("dve", lambda E, mc=mc, pp=pp, x1f=x1f:
                      E.scalar_tensor_tensor(
                tmpf[:, mc, :], pp[:, 0:T], v["bout"][:, mc:mc + 1],
                x1f[:, mc, :], ALU.add, ALU.add), inc="dve")
            ev_s1[mc] = self.cnt["dve"]
            if mc == KC // 2 - 1:
                EV[("s1h", t)] = self.cnt["dve"]
        EV[("s1d", t)] = self.cnt["dve"]
        self.wait("sync", "dve")
        self.dma("sync", lambda E, t=t: E.dma_start(out=d["s1_s"][t],
                                                    in_=tmpf[:]), "qs")

    MK.phaseA = phaseA
    MK._A_stage1a = _A_stage1a
    MK._A_stage1b = _A_stage1b
    MK._A_chain = _A_chain
    MK._A_stage2 = _A_stage2
    MK.phaseB = phaseB
    MK.phaseC = phaseC
    MK.phaseD = phaseD
    MK._D_s1a = _D_s1a
    MK._D_s1b = _D_s1b
    MK._D_s2 = _D_s2
    MK.phaseE = phaseE
    MK._ffn_up = _ffn_up
    MK._ffn_down = _ffn_down
    MK._ln_tile = ln_tile
    MK._bcast = bcast
    MK._load_weight = load_weight
    MK._LN = LN


attach_phases(MK)
